# version 28
# baseline (speedup 1.0000x reference)
import numpy as np
import ml_dtypes
from contextlib import ExitStack

import concourse.bass as bass
import concourse.mybir as mybir
from concourse.bass_core_rust import IndirectOffsetOnAxis
from concourse.bass_utils import run_bass_kernel_spmd

F32 = mybir.dt.float32
BF16 = mybir.dt.bfloat16
I32 = mybir.dt.int32
AF = mybir.ActivationFunctionType
ALU = mybir.AluOpType
AX = mybir.AxisListType.X

NC_ = 8
H = 16
HPC = 2
QG = 256
KBW = 512


class Cfg:
    def __init__(self, T, D, DE):
        self.T, self.D, self.DE = T, D, DE
        self.TL = T // NC_
        self.NT = self.TL // 128
        self.DH = D // H
        self.CH = D // 128
        self.VW = self.DH + 2
        self.NSL = max(1, D // 512)
        self.SW = min(512, D)
        self.NJE = DE // 128
        self.NEXP = 8
        self.NVE = self.NEXP + 2
        self.LNEPS = 1e-5
        self.L2EPS = 1e-8
        self.QASCL = 1.0 / float(np.sqrt(np.float32(D)))


REAL = Cfg(T=8192, D=2048, DE=1024)
TINY = Cfg(T=1024, D=512, DE=256)


class Eng:
    def __init__(self, name):
        self.name, self.ops = name, []

    def raw(self, fn):
        self.ops.append(fn)


class Sem:
    def __init__(self, name):
        self.name, self.h, self.n = name, None, 0


def inc(eng, sem, fn, k=1):
    sem.n += k
    def g(e, fn=fn, k=k, sem=sem):
        fn(e).then_inc(sem.h, k)
    eng.ops.append(g)
    return sem.n


def wait(eng, sem, tgt):
    if tgt is None or tgt <= 0:
        return
    eng.ops.append(lambda e, s=sem, t=tgt: e.wait_ge(s.h, t))


class Arena:
    def __init__(self, t, words):
        self.t, self.words, self.off, self.marks = t, words, 0, []

    def push(self):
        self.marks.append(self.off)

    def pop(self):
        self.off = self.marks.pop()

    def tile(self, parts, n, dt):
        w = n if dt in (F32, I32) else (n + 1) // 2
        w = (w + 15) & ~15
        assert self.off + w <= self.words, (
            f"arena overflow: need {w}w at {self.off}, cap {self.words}")
        ap = self.t.ap()[0:parts, self.off:self.off + w]
        self.off += w
        if dt == F32:
            return ap[:, 0:n]
        if dt == I32:
            return ap.bitcast(I32)[:, 0:n]
        return ap.bitcast(BF16)[:, 0:n]


def build(cfg, phases=6):
    c = cfg
    nc = bass.Bass("TRN2", target_bir_lowering=False, debug=False)

    def din(name, shape, dt=F32):
        return nc.dram_tensor(name, list(shape), dt, kind="ExternalInput")

    x_full = din("x_full", [c.T, c.D])
    x_loc = din("x_loc", [c.TL, c.D])
    wqh = din("wqh", [c.D, HPC * c.DH])
    wkh = din("wkh", [c.D, HPC * c.DH])
    wvh = din("wvh", [c.D, HPC * c.DH])
    tau_in = din("tau_in", [1, 1])
    nu_in = din("nu_in", [1, 1])
    WqT = din("WqT", [c.D, c.D])
    WkT = din("WkT", [c.D, c.D])
    WvT = din("WvT", [c.D, c.D])
    WqaT = din("WqaT", [c.D, c.D])
    WkaT = din("WkaT", [c.D, c.D])
    brows = din("brows", [5, c.D])
    gbrow = din("gbrow", [2, c.D])
    W_g = din("W_g", [c.D, c.NEXP])
    We_gate = din("We_gate", [c.NEXP, c.D, c.DE])
    We_up = din("We_up", [c.NEXP, c.D, c.DE])
    We_down = din("We_down", [c.NEXP, c.DE, c.D])
    Ws_gate = din("Ws_gate", [c.D, 2 * c.DE])
    Ws_up = din("Ws_up", [c.D, 2 * c.DE])
    Ws_down = din("Ws_down", [2 * c.DE, c.D])
    tri_in = din("tri_in", [128, 128], F32)
    ident_in = din("ident_in", [128, 128], F32)
    maske_in = din("maske_in", [128, 2 * QG], F32)
    masko_in = din("masko_in", [128, 4 * QG], F32)
    prefix_in = din("prefix_in", [NC_, 1], F32)
    gidx_in = din("gidx_in", [128, c.NT * NC_], I32)

    y_out = nc.dram_tensor("y", [c.TL, c.D], F32, kind="ExternalOutput")

    def scr(name, shape, dt=F32):
        return nc.dram_tensor(name, list(shape), dt)

    xs1_d = scr("xs1_d", [c.TL, c.D])
    xs2_d = scr("xs2_d", [c.TL, c.D])
    xs3_d = scr("xs3_d", [c.TL, c.D])
    acc_d = scr("acc_d", [c.TL, c.D])
    hfm_d = scr("hfm_d", [c.NT * 128, c.D], BF16)
    qfm_d = scr("qfm_d", [c.NT * 128, c.D], BF16)
    qry_d = scr("qry_d", [c.TL, c.D])
    qw_d = scr("qw_d", [c.TL, c.D])
    y1_d = scr("y1_d", [c.TL, c.D])
    key_d = scr("key_d", [c.TL, c.D])
    val_d = scr("val_d", [c.TL, c.D])
    y2_d = scr("y2_d", [c.TL, c.D])
    mix_d = scr("mix_d", [c.TL, c.D])
    kw_d = scr("kw_d", [c.TL, c.D])
    y3_d = scr("y3_d", [c.TL, c.D])
    y4_d = scr("y4_d", [c.TL, c.D])
    agin_h = [scr(f"agin{h}", [c.T, c.DH], BF16) for h in range(HPC)]
    agout_h = [scr(f"agout{h}", [NC_ * c.T, c.DH], BF16) for h in range(HPC)]
    ccin = scr("ccin", [1, c.D])
    ccout = scr("ccout", [NC_, c.D])

    SY, GP, TE, VE, SC = (Eng(n) for n in ("sync", "gpsimd", "tensor", "vector", "scalar"))
    sd, sg, scc, st, sv, ss = (Sem(n) for n in ("sd", "sg", "scc", "st", "sv", "ss"))
    sdx = [Sem("sdx0"), Sem("sdx1")]
    sgG, sgU, sgD, sga, sgi = (Sem(n) for n in ("sgG", "sgU", "sgD", "sga", "sgi"))
    scin = Sem("scin")
    scin2 = Sem("scin2")
    scar = Sem("scar")

    es = ExitStack()
    with es:
        for s in (sd, sg, scc, st, sv, ss, sdx[0], sdx[1], sgG, sgU, sgD, sga, sgi,
                  scin, scin2, scar):
            s.h = es.enter_context(nc.semaphore(s.name))

        AW = 52224
        arena_t = es.enter_context(nc.sbuf_tensor("arena", [128, AW], F32))
        A = Arena(arena_t, AW)

        tri_bf = A.tile(128, 128, BF16)
        ident_bf = A.tile(128, 128, BF16)
        prefix_sb = A.tile(NC_, 1, F32)
        gidx_sb = A.tile(128, c.NT * NC_, I32)
        ones128_bf = A.tile(128, 1, BF16)
        ones1_f = A.tile(1, 128, F32)
        ones1_b = A.tile(1, 128, BF16)
        onescol_f = A.tile(128, 1, F32)
        eps_ln = A.tile(128, 1, F32)
        tau_sb = A.tile(1, 1, F32)
        nu_tmp = A.tile(1, 1, F32)
        tau_col = A.tile(128, 1, F32)
        nu_col = A.tile(128, 1, F32)
        negabstau = A.tile(128, 1, F32)
        w8 = [A.tile(128, c.NEXP, F32) for _ in range(c.NT)]
        stat = A.tile(128, 8, F32)

        pb = [es.enter_context(nc.psum_tensor(f"pb{i}", [128, 512], F32))
              for i in range(8)]

        def pbf(i):
            return pb[i].ap().bitcast(BF16)

        block = es.enter_context(nc.Block())

        def dma(eng, sem, out, in_, **kw):
            return inc(eng, sem,
                       lambda e, o=out, i=in_, kw=kw: e.dma_start(out=o, in_=i, **kw), 16)

        def wload(eng, sem, dst2d, src2d, inner):
            s3 = src2d.rearrange("(k p) d -> p k d", p=128)
            d3 = dst2d.rearrange("p (k d) -> p k d", d=inner)
            K = s3.shape[1]
            npcs = max(1, (K * 128) // 512)
            npcs = min(npcs, K)
            step = (K + npcs - 1) // npcs
            t = None
            for k0 in range(0, K, step):
                k1 = min(K, k0 + step)
                t = dma(eng, sem, d3[:, k0:k1], s3[:, k0:k1])
            return t

        dma(GP, sg, tri_bf[:, :], tri_in[:, :])
        dma(GP, sg, ident_bf[:, :], ident_in[:, :])
        t_pfx = dma(SY, sd, prefix_sb[:, :], prefix_in[:, :])
        t_gidx = dma(SY, sd, gidx_sb[:, :], gidx_in[:, :])
        dma(SY, sd, tau_sb[:, :], tau_in[:, :])
        t_ld = dma(SY, sd, nu_tmp[:, :], nu_in[:, :])

        VE.raw(lambda e: e.memset(ones128_bf[:, :], 1.0))
        VE.raw(lambda e: e.memset(ones1_f[:, :], 1.0))
        VE.raw(lambda e: e.memset(ones1_b[:, :], 1.0))
        VE.raw(lambda e: e.memset(onescol_f[:, :], 1.0))
        inc(VE, sv, lambda e: e.memset(eps_ln[:, :], c.LNEPS))

        wait(TE, sd, t_ld)
        wait(TE, sv, sv.n)
        inc(TE, st, lambda e: e.matmul(pb[0][:, 0:1], ones1_f[:, :], tau_sb[:, :],
                                       start=True, stop=True))
        inc(TE, st, lambda e: e.matmul(pb[1][:, 0:1], ones1_f[:, :], nu_tmp[:, :],
                                       start=True, stop=True))
        t_te0 = st.n
        wait(VE, st, t_te0)
        inc(VE, sv, lambda e: e.tensor_copy(tau_col[:, :], pb[0][:, 0:1]))
        t_nu = inc(VE, sv, lambda e: e.tensor_copy(nu_col[:, :], pb[1][:, 0:1]))
        wait(SC, sv, t_nu)
        t_abs = inc(SC, ss, lambda e: e.activation(negabstau[:, :], tau_col[:, :], AF.Abs))
        wait(VE, ss, t_abs)
        inc(VE, sv, lambda e: e.tensor_scalar_mul(negabstau[:, :], negabstau[:, :], -1.0))

        def ln_tile(xap, out_bf_ap, out_f32_ap, dump_bf):
            wait(SC, sv, sv.n)
            wait(SC, ss, ss.n)
            wait(VE, ss, ss.n)
            wait(VE, sv, sv.n)
            t0 = inc(VE, sv, lambda e: e.reduce_sum(stat[:, 0:1], xap, axis=AX))
            t1 = inc(SC, ss, lambda e: e.activation(dump_bf, xap, AF.Square,
                                                    accum_out=stat[:, 1:2]))
            wait(VE, ss, t1)
            wait(VE, sv, t0)
            inc(VE, sv, lambda e: e.tensor_scalar_mul(stat[:, 2:3], stat[:, 0:1], -1.0 / c.D))
            wait(VE, sv, sv.n)
            inc(VE, sv, lambda e: e.tensor_scalar_mul(stat[:, 1:2], stat[:, 1:2], 1.0 / c.D))
            wait(VE, sv, sv.n)
            inc(VE, sv, lambda e: e.tensor_tensor(stat[:, 3:4], stat[:, 2:3], stat[:, 2:3], ALU.mult))
            wait(VE, sv, sv.n)
            t2 = inc(VE, sv, lambda e: e.tensor_tensor(stat[:, 1:2], stat[:, 1:2], stat[:, 3:4],
                                                       ALU.subtract))
            wait(SC, sv, t2)
            inc(SC, ss, lambda e: e.activation(stat[:, 3:4], stat[:, 1:2], AF.Ln, bias=eps_ln[:, :]))
            wait(SC, ss, ss.n)
            t3 = inc(SC, ss, lambda e: e.activation(stat[:, 3:4], stat[:, 3:4], AF.Exp, scale=-0.5))
            wait(VE, ss, t3)
            wait(VE, sv, t2)
            tv = None
            if out_bf_ap is not None:
                tv = inc(VE, sv, lambda e: e.tensor_scalar(
                    out_bf_ap, xap, stat[:, 2:3], stat[:, 3:4], ALU.add, ALU.mult))
            if out_f32_ap is not None:
                tv = inc(VE, sv, lambda e: e.tensor_scalar(
                    out_f32_ap, xap, stat[:, 2:3], stat[:, 3:4], ALU.add, ALU.mult))
            return tv

        def transpose_tile(src_bf, dst_fn, t_src_v, alt=True):
            wait(TE, sv, t_src_v)
            evs = [(None, 0)] * c.CH
            for ch in range(c.CH):
                bank = 6 + (ch % 2)
                if ch >= 2:
                    sem_, tgt_ = evs[ch - 2]
                    wait(TE, sem_, tgt_)
                inc(TE, st, lambda e, ch=ch, bank=bank: e.transpose(
                    pbf(bank)[:, 0:128], src_bf[:, ch * 128:(ch + 1) * 128], ident_bf[:, :]))
                if ch % 2 == 0 or not alt:
                    wait(VE, st, st.n)
                    evs[ch] = (sv, inc(VE, sv, lambda e, ch=ch, bank=bank: e.tensor_copy(
                        dst_fn(ch), pbf(bank)[:, 0:128])))
                else:
                    wait(SC, st, st.n)
                    evs[ch] = (ss, inc(SC, ss, lambda e, ch=ch, bank=bank: e.copy(
                        dst_fn(ch), pbf(bank)[:, 0:128])))
            sv_t = max((t for s_, t in evs if s_ is sv), default=0)
            ss_t = max((t for s_, t in evs if s_ is ss), default=0)
            return sv_t, ss_t

        A.push()
        maske_bf = A.tile(128, 2 * QG, BF16)
        masko_bf = A.tile(128, 4 * QG, BF16)
        dma(GP, sg, maske_bf[:, :], maske_in[:, :])
        dma(GP, sg, masko_bf[:, :], masko_in[:, :])
        Qb = [A.tile(c.DH, c.T, BF16) for _ in range(HPC)]
        Kb = [A.tile(c.DH, c.T, BF16) for _ in range(HPC)]
        Vb = [A.tile(128, (c.T // 128) * c.VW, BF16) for _ in range(HPC)]
        A.push()
        xa_t = [A.tile(128, c.D, F32) for _ in range(2)]
        lnx_bf2 = [A.tile(128, c.D, BF16) for _ in range(2)]
        lnx_fm2 = [A.tile(128, c.CH * 128, BF16) for _ in range(2)]
        qn_bf2 = [A.tile(128, 6 * c.DH, BF16) for _ in range(2)]
        fcol2 = [A.tile(128, 8, F32) for _ in range(2)]
        dmp_sq = A.tile(128, c.DH, BF16)
        wq_sb = A.tile(128, c.CH * HPC * c.DH, BF16)
        wk_sb = A.tile(128, c.CH * HPC * c.DH, BF16)
        wv_sb = A.tile(128, c.CH * HPC * c.DH, BF16)

        for h in range(HPC):
            inc(VE, sv, lambda e, h=h: e.memset(
                Vb[h].rearrange("p (s w) -> p s w", w=c.VW)[:, :, c.DH:c.DH + 1], 1.0))
        v_ones = sv.n

        t_wq = wload(GP, sg, wq_sb[:, :], wqh.ap(), HPC * c.DH)
        t_wk = wload(GP, sg, wk_sb[:, :], wkh.ap(), HPC * c.DH)
        t_wv = wload(GP, sg, wv_sb[:, :], wvh.ap(), HPC * c.DH)

        NTA = c.T // 128
        x_free = [0, 0]
        lnx_free_t = [0, 0]
        lnb_free = [0, 0]
        bank_free = [0] * 6
        fac_prev = [0, 0]
        wait(TE, sg, t_wv)
        ph_list = [(wq_sb, 0, 'q'), (wq_sb, 1, 'q'), (wk_sb, 0, 'k'),
                   (wk_sb, 1, 'k'), (wv_sb, 0, 'v'), (wv_sb, 1, 'v')]
        for ti in range(NTA):
            xi = ti % 2
            lnx_bf = lnx_bf2[xi]
            lnx_fm = lnx_fm2[xi]
            qn_bf = qn_bf2[xi]
            fcol = fcol2[xi]
            wait(SY, sv, x_free[xi])
            t_x = dma(SY, sdx[xi], xa_t[xi][:, :], x_full[ti * 128:(ti + 1) * 128, :])
            wait(VE, sdx[xi], t_x)
            wait(SC, sdx[xi], t_x)
            wait(VE, st, lnb_free[xi])
            t_ln = ln_tile(xa_t[xi][:, :], lnx_bf[:, :], None, lnx_bf[:, :])
            x_free[xi] = t_ln
            wait(TE, st, lnx_free_t[xi])
            t_fm_v, t_fm_s = transpose_tile(
                lnx_bf, lambda ch, dst=lnx_fm: dst[:, ch * 128:(ch + 1) * 128], t_ln)
            lnb_free[xi] = st.n
            wait(TE, sv, t_fm_v)
            wait(TE, ss, t_fm_s)
            if ti == 0:
                wait(TE, sv, v_ones)
            for phi, (wsb, hh, kind) in enumerate(ph_list):
                bank = phi
                wait(TE, ss, bank_free[phi])
                for ch in range(c.CH):
                    inc(TE, st, lambda e, wsb=wsb, hh=hh, ch=ch, bank=bank,
                        lnx_fm=lnx_fm: e.matmul(
                        pb[bank][:, 0:c.DH],
                        lnx_fm[:, ch * 128:(ch + 1) * 128],
                        wsb[:, ch * HPC * c.DH + hh * c.DH: ch * HPC * c.DH + (hh + 1) * c.DH],
                        start=(ch == 0), stop=(ch == c.CH - 1)))
                wait(SC, st, st.n)
                if phi == 0:
                    wait(SC, sv, fac_prev[xi])
                bank_free[phi] = inc(SC, ss, lambda e, bank=bank, phi=phi, qn_bf=qn_bf: e.copy(
                    qn_bf[:, phi * c.DH:(phi + 1) * c.DH], pb[bank][:, 0:c.DH]))
                wait(SC, ss, ss.n)
                inc(SC, ss, lambda e, phi=phi, qn_bf=qn_bf, fcol=fcol: e.activation(
                    dmp_sq[:, 0:c.DH], qn_bf[:, phi * c.DH:(phi + 1) * c.DH], AF.Square,
                    accum_out=fcol[:, phi:phi + 1]))
            sq_all = ss.n
            lnx_free_t[xi] = st.n
            wait(SC, ss, sq_all)
            inc(SC, ss, lambda e, fcol=fcol: e.activation(fcol[:, 0:6], fcol[:, 0:6], AF.Ln))
            wait(SC, ss, ss.n)
            t_f = inc(SC, ss, lambda e, fcol=fcol: e.activation(fcol[:, 0:6], fcol[:, 0:6],
                                                                AF.Exp, scale=0.5))
            wait(VE, ss, t_f)
            inc(VE, sv, lambda e, fcol=fcol: e.tensor_scalar_add(fcol[:, 0:6], fcol[:, 0:6], c.L2EPS))
            wait(VE, sv, sv.n)
            inc(VE, sv, lambda e, fcol=fcol: e.reciprocal(fcol[:, 0:6], fcol[:, 0:6]))
            wait(VE, sv, sv.n)
            inc(VE, sv, lambda e, fcol=fcol: e.tensor_scalar(fcol[:, 0:1], fcol[:, 0:1],
                                                             tau_col[:, 0:1], None, ALU.mult))
            wait(VE, sv, sv.n)
            t_fac = inc(VE, sv, lambda e, fcol=fcol: e.tensor_scalar(fcol[:, 1:2], fcol[:, 1:2],
                                                                     tau_col[:, 0:1], None, ALU.mult))
            wait(VE, sv, t_fac)
            tr_free = [0, 0]
            for phi, (wsb, hh, kind) in enumerate(ph_list):
                qs = qn_bf[:, phi * c.DH:(phi + 1) * c.DH]
                if kind == 'v':
                    dst = Vb[hh].rearrange("p (s w) -> p s w", w=c.VW)[:, ti, 0:c.DH]
                    inc(VE, sv, lambda e, qs=qs, phi=phi, dst=dst, fcol=fcol: e.tensor_scalar(
                        dst, qs, fcol[:, phi:phi + 1], None, ALU.mult))
                else:
                    t_n = inc(VE, sv, lambda e, qs=qs, phi=phi, fcol=fcol: e.tensor_scalar(
                        qs, qs, fcol[:, phi:phi + 1], None, ALU.mult))
                    tb = 6 + (phi % 2)
                    wait(TE, sv, t_n)
                    wait(TE, sv, tr_free[phi % 2])
                    inc(TE, st, lambda e, tb=tb, qs=qs: e.transpose(
                        pbf(tb)[0:c.DH, 0:128], qs, ident_bf[:, :]))
                    wait(VE, st, st.n)
                    dstfm = (Qb if kind == 'q' else Kb)[hh][0:c.DH, ti * 128:(ti + 1) * 128]
                    tr_free[phi % 2] = inc(VE, sv, lambda e, tb=tb, dstfm=dstfm: e.tensor_copy(
                        dstfm, pbf(tb)[0:c.DH, 0:128]))
            fac_prev[xi] = sv.n
        a2_v = sv.n
        A.pop()
        A.push()
        contrib = A.tile(128, (c.T // 128) * HPC * c.DH, BF16)
        p_bf = [A.tile(128, 4 * QG, BF16) for _ in range(2)]
        p_bf2_extra = A.tile(128, 4 * QG, BF16)
        l_col = A.tile(128, 2, F32)

        NQG = c.T // QG
        NPS = 3
        prev_pv = [0] * NPS
        prev_exp = [0] * NPS
        o_free = 0
        wait(TE, sv, a2_v)
        p_bf3 = p_bf + [p_bf2_extra]
        for hh in range(HPC if phases >= 2 else 0):
            for g in range(NQG):
                nfull = (g * QG) // KBW
                kblocks = [(i * KBW, 4, False) for i in range(nfull)]
                kblocks.append((nfull * KBW, 2 if g % 2 == 0 else 4, True))
                nkb = len(kblocks)
                def emit_pv(pend_):
                    i_, par_, chunks_, k0_, dsem_, dtgt_ = pend_
                    wait(TE, dsem_, dtgt_)
                    for qt in range(2):
                        for ci in range(chunks_):
                            slot = k0_ // 128 + ci
                            first = (i_ == 0 and ci == 0)
                            last = (i_ == nkb - 1 and ci == chunks_ - 1)
                            inc(TE, st, lambda e, hh=hh, qt=qt, ci=ci, par=par_, slot=slot,
                                first=first, last=last: e.matmul(
                                pb[6 + qt][:, 0:c.DH + 1],
                                p_bf3[par][:, ci * QG + qt * 128: ci * QG + qt * 128 + 128],
                                Vb[hh].rearrange("p (s w) -> p s w", w=c.VW)[:, slot, 0:c.DH + 1],
                                start=first, stop=last))
                    prev_pv[par_] = st.n

                pend = None
                for i, (k0, chunks, diag) in enumerate(kblocks):
                    par = i % NPS
                    pk0 = 2 * par
                    wait(TE, ss, prev_exp[par])
                    if i == 0:
                        wait(TE, sv, o_free)
                    for ci in range(chunks):
                        inc(TE, st, lambda e, hh=hh, g=g, k0=k0, ci=ci, pk0=pk0: e.matmul(
                            pb[pk0 + ci // 2][:, (ci % 2) * QG:(ci % 2 + 1) * QG],
                            Kb[hh][0:c.DH, k0 + ci * 128:k0 + (ci + 1) * 128],
                            Qb[hh][0:c.DH, g * QG:(g + 1) * QG],
                            start=True, stop=True))
                    t_sT = st.n
                    wait(SC, st, t_sT)
                    wait(SC, st, prev_pv[par])
                    for b in range((chunks + 1) // 2):
                        w_ = min(2 * QG, (chunks - 2 * b) * QG)
                        inc(SC, ss, lambda e, b=b, w_=w_, par=par, pk0=pk0: e.activation(
                            p_bf3[par][:, b * 2 * QG: b * 2 * QG + w_],
                            pb[pk0 + b][:, 0:w_], AF.Exp, bias=negabstau[:, :]))
                    prev_exp[par] = ss.n
                    dep_sem, dep_tgt = ss, ss.n
                    if diag:
                        wait(VE, ss, ss.n)
                        m = maske_bf if g % 2 == 0 else masko_bf
                        dep_tgt = inc(VE, sv, lambda e, m=m, par=par, chunks=chunks: e.tensor_tensor(
                            p_bf3[par][:, 0:chunks * QG], p_bf3[par][:, 0:chunks * QG],
                            m[:, 0:chunks * QG], ALU.mult))
                        dep_sem = sv
                    if pend is not None:
                        emit_pv(pend)
                    pend = (i, par, chunks, k0, dep_sem, dep_tgt)
                emit_pv(pend)
                wait(VE, st, st.n)
                for qt in range(2):
                    inc(VE, sv, lambda e, qt=qt: e.reciprocal(
                        l_col[:, qt:qt + 1], pb[6 + qt][:, c.DH:c.DH + 1]))
                    wait(VE, sv, sv.n)
                    inc(VE, sv, lambda e, qt=qt: e.tensor_tensor(
                        l_col[:, qt:qt + 1], l_col[:, qt:qt + 1], nu_col[:, :], ALU.mult))
                    wait(VE, sv, sv.n)
                    gt = g * 2 + qt
                    dst = contrib[:, gt * HPC * c.DH + hh * c.DH: gt * HPC * c.DH + (hh + 1) * c.DH]
                    inc(VE, sv, lambda e, qt=qt, dst=dst: e.tensor_scalar(
                        dst, pb[6 + qt][:, 0:c.DH], l_col[:, qt:qt + 1], None, ALU.mult))
                o_free = sv.n
            wait(SY, sv, sv.n)
            _ag3 = agin_h[hh].ap().rearrange("(g p) f -> p g f", p=128)
            _ct4 = contrib.rearrange("p (g h f) -> p g h f", h=HPC, f=c.DH)
            NGW = c.T // 128
            spillsem = scin if hh == 0 else scin2
            t_cw = None
            for g0 in range(0, NGW, 8):
                g1 = min(NGW, g0 + 8)
                t_cw = dma(SY, spillsem, _ag3[:, g0:g1], _ct4[:, g0:g1, hh, :])
            wait(GP, spillsem, t_cw)
            t_ag1 = inc(GP, scc, lambda e, hh=hh: e.collective_compute(
                "AllGather", ALU.bypass, replica_groups=[list(range(NC_))],
                ins=[agin_h[hh].ap().opt()], outs=[agout_h[hh].ap().opt()]))
        if phases < 2:
            t_ag1 = 0
        A.pop()
        A.pop()

        lnx_reg = [A.tile(128, c.TL, BF16) for _ in range(c.CH)]

        A.push()
        att_t2 = [A.tile(128, HPC * c.DH * NC_, BF16) for _ in range(2)]
        xr_t2 = [A.tile(128, c.D, F32) for _ in range(2)]
        xs_t2 = [A.tile(128, c.D, F32) for _ in range(2)]
        lnt_bf2 = [A.tile(128, c.D, BF16) for _ in range(2)]

        wait(GP, scc, t_ag1)
        wait(GP, sd, t_ld)
        bfv = [0, 0]
        bfs = [0, 0]
        bfd = [0, 0]
        for t in range(c.NT if phases >= 3 else 0):
            par = t % 2
            att_t, xr_t = att_t2[par], xr_t2[par]
            xs_t, lnt_bf = xs_t2[par], lnt_bf2[par]
            wait(GP, sv, bfv[par])
            wait(GP, ss, bfs[par])
            for r in range(NC_):
                col = t * NC_ + r
                for h in range(HPC):
                    t_att = inc(GP, sgi, lambda e, r=r, col=col, h=h, att_t=att_t:
                                e.indirect_dma_start(
                        att_t[:, r * HPC * c.DH + h * c.DH:
                              r * HPC * c.DH + (h + 1) * c.DH], None,
                        agout_h[h].ap(),
                        IndirectOffsetOnAxis(gidx_sb[:, col:col + 1], axis=0)), 16)
            wait(SY, sv, bfv[par])
            wait(SY, ss, bfs[par])
            t_xl = dma(SY, sd, xr_t[:, :], x_loc[t * 128:(t + 1) * 128, :])
            wait(VE, sgi, t_att)
            wait(VE, sd, t_xl)
            wait(VE, sd, bfd[par])
            t_add = inc(VE, sv, lambda e, xs_t=xs_t, xr_t=xr_t, att_t=att_t:
                        e.tensor_tensor(xs_t[:, :], xr_t[:, :], att_t[:, :], ALU.add))
            wait(SY, sv, t_add)
            t_w1 = dma(SY, sd, xs1_d[t * 128:(t + 1) * 128, :], xs_t[:, :])
            wait(SC, sv, t_add)
            t_ln2 = ln_tile(xs_t[:, :], lnt_bf[:, :], None, lnt_bf[:, :])
            transpose_tile(lnt_bf, lambda ch, t=t: lnx_reg[ch][:, t * 128:(t + 1) * 128], t_ln2)
            bfv[par] = sv.n
            bfs[par] = ss.n
            bfd[par] = t_w1
        a4_v = sv.n
        a4_s = ss.n
        wait(SY, sd, sd.n)
        A.pop()

        def ffn_phase(lnx_dep_v, lnx_dep_s, xs_in_d, xs_out_d):
            A.push()
            wg_sb = A.tile(128, c.CH * c.NEXP, BF16)
            r8 = A.tile(128, c.NEXP, F32)
            rst = A.tile(128, 4, F32)
            slotG_off = A.off
            slotG = A.tile(128, c.CH * c.DE, BF16)
            slotU = A.tile(128, c.CH * c.DE, BF16)
            slotD = A.tile(128, c.NJE * c.D, BF16)
            acc = A.tile(128, c.NT * c.D, F32)
            TSW = min(512, c.TL)
            NTS = c.TL // TSW
            act_off = A.off
            act_bf = A.tile(128, c.NJE * 512, BF16)
            xsb = A.t.ap()[0:128, act_off:act_off + c.D]

            wait(GP, st, st.n)
            wait(GP, sv, sv.n)
            wait(GP, ss, ss.n)
            wait(GP, sd, sd.n)
            t_wg = wload(GP, sgG, wg_sb[:, :], W_g.ap(), c.NEXP)

            def vexp(e):
                if e < c.NEXP:
                    return (We_gate.ap()[e], We_up.ap()[e], We_down.ap()[e], 'r', e)
                hh = e - c.NEXP
                return (Ws_gate.ap()[:, hh * c.DE:(hh + 1) * c.DE],
                        Ws_up.ap()[:, hh * c.DE:(hh + 1) * c.DE],
                        Ws_down.ap()[hh * c.DE:(hh + 1) * c.DE, :], 's', hh)

            def load_slot(slot_sb, sem, src_ap, free_t, inner):
                wait(GP, st, free_t)
                return wload(GP, sem, slot_sb[:, :], src_ap, inner)

            slotG3 = slotG.rearrange("p (k d) -> p k d", d=c.DE)
            slotU3 = slotU.rearrange("p (k d) -> p k d", d=c.DE)

            def chunk_load(eng, slot3, sem, src2, jc, free_t):
                wait(eng, st, free_t)
                s3 = src2.rearrange("(k p) d -> p k d", p=128)
                return dma(eng, sem, slot3[:, :, jc * 128:(jc + 1) * 128],
                           s3[:, :, jc * 128:(jc + 1) * 128])

            tokG, tokU, tokD = {}, {}, {}
            g0, u0, d0, _, _ = vexp(0)
            t_g0 = load_slot(slotG, sgG, g0, 0, c.DE)
            t_u0 = load_slot(slotU, sgU, u0, 0, c.DE)
            tokD[0] = load_slot(slotD, sgD, d0, 0, c.D)
            for jc in range(c.NJE):
                tokG[(0, jc)] = t_g0
                tokU[(0, jc)] = t_u0
            wait(TE, sgG, t_g0)
            wait(TE, sv, lnx_dep_v)
            wait(TE, ss, lnx_dep_s)
            for t in range(c.NT):
                for ch in range(c.CH):
                    inc(TE, st, lambda e, t=t, ch=ch: e.matmul(
                        pb[6][:, 0:c.NEXP], lnx_reg[ch][:, t * 128:(t + 1) * 128],
                        wg_sb[:, ch * c.NEXP:(ch + 1) * c.NEXP],
                        start=(ch == 0), stop=(ch == c.CH - 1)))
                wait(VE, st, st.n)
                wait(VE, ss, ss.n)
                inc(VE, sv, lambda e: e.reduce_max(rst[:, 0:1], pb[6][:, 0:c.NEXP],
                                                   axis=AX, negate=True))
                wait(SC, sv, sv.n)
                t_e = inc(SC, ss, lambda e: e.activation(r8[:, :], pb[6][:, 0:c.NEXP],
                                                         AF.Exp, bias=rst[:, 0:1],
                                                         accum_out=rst[:, 1:2]))
                wait(VE, ss, t_e)
                inc(VE, sv, lambda e: e.reciprocal(rst[:, 2:3], rst[:, 1:2]))
                wait(VE, sv, sv.n)
                inc(VE, sv, lambda e: e.tensor_scalar(r8[:, :], r8[:, :], rst[:, 2:3], None, ALU.mult))
                wait(VE, sv, sv.n)
                inc(VE, sv, lambda e: e.reduce_max(rst[:, 0:1], r8[:, :], axis=AX))
                wait(VE, sv, sv.n)
                inc(VE, sv, lambda e, t=t: e.tensor_scalar(w8[t][:, :], r8[:, :], rst[:, 0:1],
                                                           None, ALU.is_ge))
                wait(VE, sv, sv.n)
                inc(VE, sv, lambda e, t=t: e.scalar_tensor_tensor(
                    w8[t][:, :], w8[t][:, :], -2.0, r8[:, :], ALU.mult, ALU.add))
                wait(VE, sv, sv.n)
                inc(VE, sv, lambda e, t=t: e.reduce_max(rst[:, 1:2], w8[t][:, :], axis=AX))
                wait(VE, sv, sv.n)
                inc(VE, sv, lambda e, t=t: e.tensor_scalar(w8[t][:, :], r8[:, :], rst[:, 1:2],
                                                           None, ALU.is_ge))
                wait(VE, sv, sv.n)
                t_w8 = inc(VE, sv, lambda e, t=t: e.tensor_tensor(w8[t][:, :], r8[:, :],
                                                                  w8[t][:, :], ALU.mult))
                wait(TE, sv, t_w8)
            router_v = sv.n

            act_par = [0, 0]
            dbank_tok = [0, 0]
            for e_i in range(c.NVE):
                gap_, uap_, dap_, wk_, wi_ = vexp(e_i)
                if e_i + 1 < c.NVE:
                    gn, un, _, _, _ = vexp(e_i + 1)
                for tsl in range(NTS):
                    t0 = tsl * TSW
                    act_v = 0
                    for jc in range(c.NJE):
                        gb, ub = (0, 1) if jc % 2 == 0 else (4, 5)
                        if tsl == 0:
                            wait(TE, sgG, tokG[(e_i, jc)])
                            wait(TE, sgU, tokU[(e_i, jc)])
                            if e_i == 0 and jc == 0:
                                wait(TE, sv, router_v)
                        if jc >= 2:
                            wait(TE, sv, act_par[jc % 2])
                        for ch in range(c.CH):
                            inc(TE, st, lambda e, jc=jc, ch=ch, t0=t0, gb=gb: e.matmul(
                                pb[gb][:, 0:TSW],
                                slotG[:, ch * c.DE + jc * 128: ch * c.DE + (jc + 1) * 128],
                                lnx_reg[ch][:, t0:t0 + TSW],
                                start=(ch == 0), stop=(ch == c.CH - 1)))
                        for ch in range(c.CH):
                            inc(TE, st, lambda e, jc=jc, ch=ch, t0=t0, ub=ub: e.matmul(
                                pb[ub][:, 0:TSW],
                                slotU[:, ch * c.DE + jc * 128: ch * c.DE + (jc + 1) * 128],
                                lnx_reg[ch][:, t0:t0 + TSW],
                                start=(ch == 0), stop=(ch == c.CH - 1)))
                        t_gu = st.n
                        if tsl == NTS - 1 and e_i + 1 < c.NVE:
                            tokG[(e_i + 1, jc)] = chunk_load(GP, slotG3, sgG, gn, jc, t_gu)
                            tokU[(e_i + 1, jc)] = chunk_load(GP, slotU3, sgU, un, jc, t_gu)
                        wait(SC, st, t_gu)
                        wait(SC, sv, act_par[jc % 2])
                        t_si = inc(SC, ss, lambda e, jc=jc, gb=gb: e.activation(
                            act_bf[:, jc * 512: jc * 512 + TSW], pb[gb][:, 0:TSW], AF.Silu))
                        wait(VE, ss, t_si)
                        act_v = inc(VE, sv, lambda e, jc=jc, ub=ub: e.tensor_tensor(
                            act_bf[:, jc * 512: jc * 512 + TSW],
                            act_bf[:, jc * 512: jc * 512 + TSW],
                            pb[ub][:, 0:TSW], ALU.mult))
                        act_par[jc % 2] = act_v
                    wait(TE, sgD, tokD[e_i])
                    wait(TE, sv, act_v)
                    for ts2 in range(TSW // 128):
                        tt = tsl * (TSW // 128) + ts2
                        for dsl in range(c.NSL):
                            bank = 2 + (dsl % 2)
                            wait(TE, sv, dbank_tok[dsl % 2])
                            for jc in range(c.NJE):
                                inc(TE, st, lambda e, ts2=ts2, dsl=dsl, jc=jc, bank=bank: e.matmul(
                                    pb[bank][:, 0:c.SW],
                                    act_bf[:, jc * 512 + ts2 * 128: jc * 512 + ts2 * 128 + 128],
                                    slotD[:, jc * c.D + dsl * c.SW: jc * c.D + (dsl + 1) * c.SW],
                                    start=(jc == 0), stop=(jc == c.NJE - 1)))
                            wait(VE, st, st.n)
                            wcol = w8[tt][:, wi_:wi_ + 1] if wk_ == 'r' else onescol_f[:, 0:1]
                            aslice = acc[:, tt * c.D + dsl * c.SW: tt * c.D + (dsl + 1) * c.SW]
                            if e_i == 0:
                                inc(VE, sv, lambda e, bank=bank, wcol=wcol, aslice=aslice:
                                    e.tensor_scalar(aslice, pb[bank][:, 0:c.SW],
                                                    wcol, None, ALU.mult))
                            else:
                                inc(VE, sv, lambda e, bank=bank, wcol=wcol, aslice=aslice:
                                    e.scalar_tensor_tensor(aslice, pb[bank][:, 0:c.SW],
                                                           wcol, aslice, ALU.mult, ALU.add))
                            dbank_tok[dsl % 2] = sv.n
                if e_i + 1 < c.NVE:
                    _, _, dn, _, _ = vexp(e_i + 1)
                    tokD[e_i + 1] = load_slot(slotD, sgD, dn, st.n, c.D)

            acc_v = sv.n
            xsb2 = [xsb, A.t.ap()[0:128, slotG_off:slotG_off + c.D]]
            prev_w = [0, 0]
            wait(SY, sv, acc_v)
            wait(SY, st, st.n)
            wait(SY, ss, ss.n)
            wait(SY, sd, sd.n)
            for t in range(c.NT):
                xb = xsb2[t % 2]
                wait(SY, sd, prev_w[t % 2])
                t_x = dma(SY, sd, xb, xs_in_d[t * 128:(t + 1) * 128, :])
                wait(VE, sd, t_x)
                t_s = inc(VE, sv, lambda e, t=t, xb=xb: e.tensor_tensor(
                    xb, xb, acc[:, t * c.D:(t + 1) * c.D], ALU.add))
                wait(SY, sv, t_s)
                prev_w[t % 2] = dma(SY, sd, xs_out_d[t * 128:(t + 1) * 128, :], xb)
            wait(SY, sd, sd.n)
            wait(VE, sd, sd.n)
            A.pop()

        if phases >= 4:
            ffn_phase(a4_v, a4_s, xs1_d, xs2_d)

        A.push()
        wbig = A.tile(128, c.CH * c.D, BF16)
        hfm2 = [A.tile(128, c.CH * 128, BF16) for _ in range(2)]
        wf2 = [A.tile(128, c.D, F32) for _ in range(2)]
        w2f2 = [A.tile(128, c.D, F32) for _ in range(2)]
        wbf2 = [A.tile(128, c.D, BF16) for _ in range(2)]
        carry_sb = A.tile(NC_, c.D, F32)
        gam_rep = A.tile(128, c.D, BF16)
        bet_rep = A.tile(128, c.D, BF16)
        brep = A.tile(128, c.D, BF16)
        tloc_row = A.tile(1, c.D, F32)
        carry_row = A.tile(1, c.D, F32)
        bqa_rb = A.tile(1, c.D, BF16)
        bka_rb = A.tile(1, c.D, BF16)
        tloc_rb = A.tile(1, c.D, BF16)
        carry_rb = A.tile(1, c.D, BF16)

        def load_wbig(src_, wfree=None):
            if wfree is None:
                wait(GP, st, st.n)
                wait(GP, ss, ss.n)
                wait(GP, sv, sv.n)
                wait(GP, sga, sga.n)
            s3 = src_.ap().rearrange("(k p) d -> p k d", p=128)
            d3 = wbig.rearrange("p (k d) -> p k d", d=c.D)
            toks = []
            for sl in range(c.NSL):
                if wfree is not None:
                    wait(GP, st, wfree[sl])
                toks.append(dma(GP, sg, d3[:, :, sl * c.SW:(sl + 1) * c.SW],
                                s3[:, :, sl * c.SW:(sl + 1) * c.SW]))
            return toks

        import os as _os
        KB16 = int(_os.environ.get("KB16", "5"))
        KPAR = int(_os.environ.get("KPAR", "1"))

        rowtok = {"tloc_free": 0}

        def crow_ap(i):
            return carry_row[:, :]

        def cloc_ap(i):
            return tloc_row[:, :]

        def fill_rep(rep, srcrow):
            wait(SY, st, st.n)
            wait(SY, sv, sv.n)
            wait(SY, ss, ss.n)
            wait(SY, sd, sd.n)
            wait(SY, scin, scin.n)
            t_row = dma(SY, sd, tloc_row[:, :], srcrow)
            wait(TE, sd, t_row)
            for k in range(c.NSL):
                s0 = k * c.SW
                wait(TE, sv, sv.n)
                inc(TE, st, lambda e, s0=s0: e.matmul(
                    pb[2][:, 0:c.SW], ones1_f[:, :], tloc_row[0:1, s0:s0 + c.SW],
                    start=True, stop=True))
                wait(VE, st, st.n)
                inc(VE, sv, lambda e, s0=s0, rep=rep: e.tensor_copy(
                    rep[:, s0:s0 + c.SW], pb[2][:, 0:c.SW]))

        NTB = c.NT if phases >= 5 else 0
        if phases >= 5:
            wait(SY, st, st.n)
            wait(SY, sv, sv.n)
            wait(SY, ss, ss.n)
            wait(SY, sga, sga.n)
            t_r1 = dma(SY, sd, tloc_row[:, :], brows[3:4, :])
            wait(VE, sd, t_r1)
            t_c1b = inc(VE, sv, lambda e: e.tensor_copy(bqa_rb[:, :], tloc_row[:, :]))
            wait(SY, sv, t_c1b)
            t_r2 = dma(SY, sd, tloc_row[:, :], brows[4:5, :])
            wait(VE, sd, t_r2)
            inc(VE, sv, lambda e: e.tensor_copy(bka_rb[:, :], tloc_row[:, :]))
            fill_rep(gam_rep, gbrow[0:1, :])
            fill_rep(bet_rep, gbrow[1:2, :])

        pbank_v = [0, 0, 0, 0]
        b7tok = {"v": 0, "s": 0}
        _B0_FENCE = True
        pbank_s = [0, 0, 0, 0]
        pdone = [dict(v=0, s=0, d=0), dict(v=0, s=0, d=0)]

        def pwaits(par):
            wait(SY, sv, pdone[par]["v"])
            wait(SY, ss, pdone[par]["s"])
            wait(SY, sd, pdone[par]["d"])

        def pmark(par):
            pdone[par]["v"] = sv.n
            pdone[par]["s"] = ss.n
            pdone[par]["d"] = sd.n

        def pass_fence():
            for par in (0, 1):
                pdone[par]["v"] = sv.n
                pdone[par]["s"] = ss.n
                pdone[par]["d"] = sd.n

        if phases >= 5:
            pass_fence()
        ldt = {}

        def b0_load(t):
            par = (t % 2) if KPAR else 0
            pwaits(par)
            ldt[t] = dma(SY, sd, wf2[par][:, :], xs2_d[t * 128:(t + 1) * 128, :])
        if NTB:
            b0_load(0)
        for t in range(NTB):
            par = (t % 2) if KPAR else 0
            wf, w2f, wbf, hf = wf2[par], w2f2[par], wbf2[par], hfm2[par]
            t_x = ldt[t]
            wait(VE, sd, t_x)
            wait(SC, sd, t_x)
            t_l3 = ln_tile(wf[:, :], None, w2f[:, :], wbf[:, :])
            wait(SC, sv, t_l3)
            t_l3b = ln_tile(w2f[:, :], None, wf[:, :], wbf[:, :])
            wait(VE, sv, t_l3b)
            inc(VE, sv, lambda e, wf=wf: e.tensor_tensor(wf[:, :], wf[:, :], gam_rep[:, :], ALU.mult))
            wait(VE, sv, sv.n)
            t_h = inc(VE, sv, lambda e, wf=wf, wbf=wbf: e.tensor_tensor(
                wbf[:, :], wf[:, :], bet_rep[:, :], ALU.add))
            t_hf_v, t_hf_s = transpose_tile(wbf, lambda ch, hf=hf: hf[:, ch * 128:(ch + 1) * 128],
                                            t_h, alt=bool(KB16 & 8))
            b7tok["v"], b7tok["s"] = t_hf_v, t_hf_s
            if t + 1 < NTB:
                b0_load(t + 1)
            wait(SY, sv, t_hf_v)
            wait(SY, ss, t_hf_s)
            t_hw = dma(SY, sd, hfm_d[t * 128:(t + 1) * 128, :], hf[:, :])
            pmark(par)

        def proj_from(hbuf, brep_, wslot, dep_sd, out_f, wtoks=None, rec=None):
            lastw = 0
            for sl in range(c.NSL):
                bank = sl % 4
                if wtoks is not None:
                    wait(TE, sg, wtoks[sl])
                wait(TE, sd, dep_sd)
                wait(TE, sv, pbank_v[bank])
                wait(TE, ss, pbank_s[bank])
                for ch in range(c.CH):
                    inc(TE, st, lambda e, sl=sl, ch=ch, bank=bank, hbuf=hbuf, wslot=wslot: e.matmul(
                        pb[bank][:, 0:c.SW],
                        hbuf[:, ch * 128:(ch + 1) * 128],
                        wslot[:, ch * c.D + sl * c.SW: ch * c.D + sl * c.SW + c.SW],
                        start=(ch == 0), stop=(ch == c.CH - 1)))
                if rec is not None:
                    rec[sl] = st.n
                wait(VE, st, st.n)
                lastw = inc(VE, sv, lambda e, sl=sl, bank=bank, out_f=out_f, brep_=brep_:
                            e.tensor_tensor(out_f[:, sl * c.SW:(sl + 1) * c.SW],
                                            pb[bank][:, 0:c.SW],
                                            brep_[:, sl * c.SW:(sl + 1) * c.SW], ALU.add))
                pbank_v[bank] = lastw
            return lastw

        def cumsum_tile(src_bf, ydst, cl_i, dep_v, first_tile):
            lastv = lasts = 0
            for sl in range(c.NSL):
                wait(TE, sv, dep_v)
                wait(TE, sv, lastv)
                wait(TE, ss, lasts)
                inc(TE, st, lambda e, sl=sl, src_bf=src_bf: e.matmul(
                    pb[4][:, 0:c.SW], tri_bf[:, :], src_bf[:, sl * c.SW:(sl + 1) * c.SW],
                    start=True, stop=first_tile))
                if not first_tile:
                    if KB16 & 1:
                        inc(TE, st, lambda e, sl=sl: e.matmul(
                            pb[4][:, 0:c.SW], ones1_b[:, :],
                            tloc_rb[:, sl * c.SW:(sl + 1) * c.SW],
                            start=False, stop=True))
                    else:
                        inc(TE, st, lambda e, sl=sl: e.matmul(
                            pb[4][:, 0:c.SW], ones1_f[:, :],
                            cloc_ap(0)[:, sl * c.SW:(sl + 1) * c.SW],
                            start=False, stop=True))
                t_y = st.n
                inc(TE, st, lambda e, sl=sl, src_bf=src_bf: e.matmul(
                    pb[5][0:1, 0:c.SW], ones128_bf[:, :], src_bf[:, sl * c.SW:(sl + 1) * c.SW],
                    start=True, stop=True))
                t_cs = st.n
                wait(SC, st, t_y)
                lasts = inc(SC, ss, lambda e, sl=sl, ydst=ydst: e.copy(
                    ydst[:, sl * c.SW:(sl + 1) * c.SW], pb[4][:, 0:c.SW]))
                wait(VE, st, t_cs)
                if first_tile:
                    wait(VE, scin, rowtok["tloc_free"])
                    lastv = inc(VE, sv, lambda e, sl=sl: e.tensor_copy(
                        cloc_ap(0)[:, sl * c.SW:(sl + 1) * c.SW], pb[5][0:1, 0:c.SW]))
                else:
                    lastv = inc(VE, sv, lambda e, sl=sl: e.tensor_tensor(
                        cloc_ap(0)[:, sl * c.SW:(sl + 1) * c.SW],
                        cloc_ap(0)[:, sl * c.SW:(sl + 1) * c.SW],
                        pb[5][0:1, 0:c.SW], ALU.add))
                wait(VE, sv, lastv)
                lastv = inc(VE, sv, lambda e, sl=sl: e.tensor_copy(
                    tloc_rb[:, sl * c.SW:(sl + 1) * c.SW],
                    cloc_ap(0)[:, sl * c.SW:(sl + 1) * c.SW]))
            return lasts, lastv

        def carry_round(cl_i, dep_v):
            wait(SY, sv, dep_v)
            wait(SY, scin, scin.n)
            t_w = dma(SY, scin, ccin[:, :], cloc_ap(cl_i))
            rowtok["tloc_free"] = t_w
            wait(GP, scin, t_w)
            t_cc = inc(GP, scc, lambda e: e.collective_compute(
                "AllGather", ALU.bypass, replica_groups=[list(range(NC_))],
                ins=[ccin.ap().opt()], outs=[ccout.ap().opt()]))
            wait(SY, scc, t_cc)
            t_r = dma(SY, scar, carry_sb[:, :], ccout[:, :])
            wait(TE, scar, t_r)
            wait(TE, sd, t_ld)
            last = 0
            for sl in range(c.NSL):
                wait(TE, ss, last)
                inc(TE, st, lambda e, sl=sl: e.matmul(
                    pb[7][0:1, 0:c.SW], prefix_sb[:, :], carry_sb[:, sl * c.SW:(sl + 1) * c.SW],
                    start=True, stop=True))
                wait(SC, st, st.n)
                last = inc(SC, ss, lambda e, sl=sl, cl_i=cl_i: e.copy(
                    crow_ap(cl_i)[:, sl * c.SW:(sl + 1) * c.SW], pb[7][0:1, 0:c.SW]))
                wait(SC, ss, last)
                last = inc(SC, ss, lambda e, sl=sl: e.copy(
                    carry_rb[:, sl * c.SW:(sl + 1) * c.SW],
                    crow_ap(cl_i)[:, sl * c.SW:(sl + 1) * c.SW]))
            b7tok["s"] = last
            return last

        def add_carry(y_f32, cl_i, dep_s, dep_sd_v):
            lastv = 0
            wait(TE, sv, b7tok["v"])
            wait(TE, ss, b7tok["s"])
            for sl in range(c.NSL):
                wait(TE, ss, dep_s)
                wait(TE, sv, lastv)
                if KB16 & 2:
                    inc(TE, st, lambda e, sl=sl: e.matmul(
                        pb[7][:, 0:c.SW], ones1_b[:, :], carry_rb[:, sl * c.SW:(sl + 1) * c.SW],
                        start=True, stop=True))
                else:
                    inc(TE, st, lambda e, sl=sl: e.matmul(
                        pb[7][:, 0:c.SW], ones1_f[:, :], crow_ap(0)[:, sl * c.SW:(sl + 1) * c.SW],
                        start=True, stop=True))
                wait(VE, st, st.n)
                wait(VE, sd, dep_sd_v)
                lastv = inc(VE, sv, lambda e, sl=sl, y_f32=y_f32: e.tensor_tensor(
                    y_f32[:, sl * c.SW:(sl + 1) * c.SW], y_f32[:, sl * c.SW:(sl + 1) * c.SW],
                    pb[7][:, 0:c.SW], ALU.add))
            b7tok["v"] = lastv
            return lastv

        if phases >= 5:
            fill_rep(brep, brows[0:1, :])
        t_wl = load_wbig(WqT) if phases >= 5 else None
        wq_free = [0] * c.NSL
        wqa_free = [0] * c.NSL
        wk_free = [0] * c.NSL
        wv_free = [0] * c.NSL
        pass_fence()

        def b1a_load(t):
            par = (t % 2) if KPAR else 0
            pwaits(par)
            ldt[t] = dma(SY, sd, hfm2[par][:, :], hfm_d[t * 128:(t + 1) * 128, :])
        if NTB:
            b1a_load(0)
        for t in range(NTB):
            par = (t % 2) if KPAR else 0
            wf, wbf, hf = wf2[par], wbf2[par], hfm2[par]
            t_h = ldt[t]
            t_q = proj_from(hf, brep, wbig, t_h, wf, t_wl,
                            rec=(wq_free if t == NTB - 1 else None))
            if t + 1 < NTB:
                b1a_load(t + 1)
            wait(SY, sv, t_q)
            t_qs = dma(SY, sd, qry_d[t * 128:(t + 1) * 128, :], wf[:, :])
            wait(VE, sv, t_q)
            t_qb = inc(VE, sv, lambda e, wf=wf, wbf=wbf: e.tensor_copy(wbf[:, :], wf[:, :]))
            t_fm_v, t_fm_s = transpose_tile(wbf, lambda ch, hf=hf: hf[:, ch * 128:(ch + 1) * 128],
                                            t_qb, alt=bool(KB16 & 16))
            b7tok["v"], b7tok["s"] = t_fm_v, t_fm_s
            wait(SY, sv, t_fm_v)
            wait(SY, ss, t_fm_s)
            t_qf = dma(SY, sd, qfm_d[t * 128:(t + 1) * 128, :], hf[:, :])
            pmark(par)

        t_wl = load_wbig(WqaT, wq_free) if phases >= 5 else None
        pass_fence()

        def b1b_load(t):
            par = (t % 2) if KPAR else 0
            pwaits(par)
            ldt[t] = dma(SY, sd, hfm2[par][:, :], qfm_d[t * 128:(t + 1) * 128, :])
        if NTB:
            b1b_load(0)
        cs_s = cs_v = 0
        for t in range(NTB):
            par = (t % 2) if KPAR else 0
            wf, w2f, wbf, hf = wf2[par], w2f2[par], wbf2[par], hfm2[par]
            t_h = ldt[t]
            lastw = 0
            for sl in range(c.NSL):
                bank = sl % 4
                if t_wl is not None:
                    wait(TE, sg, t_wl[sl])
                wait(TE, sd, t_h)
                wait(TE, sv, pbank_v[bank])
                wait(TE, ss, pbank_s[bank])
                for ch in range(c.CH):
                    inc(TE, st, lambda e, sl=sl, ch=ch, bank=bank, hf=hf: e.matmul(
                        pb[bank][:, 0:c.SW], hf[:, ch * 128:(ch + 1) * 128],
                        wbig[:, ch * c.D + sl * c.SW: ch * c.D + sl * c.SW + c.SW],
                        start=(ch == 0), stop=False))
                inc(TE, st, lambda e, sl=sl, bank=bank: e.matmul(
                    pb[bank][:, 0:c.SW], ones1_b[:, :],
                    bqa_rb[0:1, sl * c.SW:(sl + 1) * c.SW],
                    start=False, stop=True))
                if t == NTB - 1:
                    wqa_free[sl] = st.n
                wait(SC, st, st.n)
                lastw = inc(SC, ss, lambda e, sl=sl, bank=bank, wf=wf: e.activation(
                    wf[:, sl * c.SW:(sl + 1) * c.SW], pb[bank][:, 0:c.SW],
                    AF.Exp, scale=c.QASCL))
                pbank_s[bank] = lastw
            if t + 1 < NTB:
                b1b_load(t + 1)
            wait(SY, ss, lastw)
            t_qw = dma(SY, sd, qw_d[t * 128:(t + 1) * 128, :], wf[:, :])
            wait(VE, ss, lastw)
            t_qwb = inc(VE, sv, lambda e, wf=wf, wbf=wbf: e.tensor_copy(wbf[:, :], wf[:, :]))
            cs_s, cs_v = cumsum_tile(wbf, w2f, 0, t_qwb, t == 0)
            wait(SY, ss, cs_s)
            t_y1 = dma(SY, sd, y1_d[t * 128:(t + 1) * 128, :], w2f[:, :])
            pmark(par)
        t_c1 = carry_round(0, cs_v) if phases >= 5 else 0

        if phases >= 5:
            fill_rep(brep, brows[1:2, :])
        t_wl = load_wbig(WkT, wqa_free) if phases >= 5 else None
        pass_fence()

        def bkv_load(t):
            par = (t % 2) if KPAR else 0
            pwaits(par)
            ldt[t] = dma(SY, sd, hfm2[par][:, :], hfm_d[t * 128:(t + 1) * 128, :])
        if NTB:
            bkv_load(0)
        for t in range(NTB):
            par = (t % 2) if KPAR else 0
            wf, hf = wf2[par], hfm2[par]
            t_h = ldt[t]
            t_k = proj_from(hf, brep, wbig, t_h, wf, t_wl,
                            rec=(wk_free if t == NTB - 1 else None))
            if t + 1 < NTB:
                bkv_load(t + 1)
            wait(SY, sv, t_k)
            t_ks = dma(SY, sd, key_d[t * 128:(t + 1) * 128, :], wf[:, :])
            pmark(par)
        if phases >= 5:
            fill_rep(brep, brows[2:3, :])
        t_wl = load_wbig(WvT, wk_free) if phases >= 5 else None
        pass_fence()

        def bkv_load(t):
            par = (t % 2) if KPAR else 0
            pwaits(par)
            ldt[t] = dma(SY, sd, hfm2[par][:, :], hfm_d[t * 128:(t + 1) * 128, :])
        if NTB:
            bkv_load(0)
        for t in range(NTB):
            par = (t % 2) if KPAR else 0
            wf, hf = wf2[par], hfm2[par]
            t_h = ldt[t]
            t_k = proj_from(hf, brep, wbig, t_h, wf, t_wl,
                            rec=(wv_free if t == NTB - 1 else None))
            if t + 1 < NTB:
                bkv_load(t + 1)
            wait(SY, sv, t_k)
            t_ks = dma(SY, sd, val_d[t * 128:(t + 1) * 128, :], wf[:, :])
            pmark(par)

        pass_fence()

        def b3_load(t):
            par = (t % 2) if KPAR else 0
            pwaits(par)
            a = dma(SY, sd, wf2[par][:, :], qw_d[t * 128:(t + 1) * 128, :])
            b = dma(SY, sd, w2f2[par][:, :], y1_d[t * 128:(t + 1) * 128, :])
            ldt[t] = (a, b)
        if NTB:
            b3_load(0)
        cs_s = cs_v = 0
        for t in range(NTB):
            par = (t % 2) if KPAR else 0
            wf, w2f, wbf = wf2[par], w2f2[par], wbf2[par]
            t_qw, t_y1 = ldt[t]
            t_yg = add_carry(w2f, 0, t_c1, t_y1)
            wait(SC, sv, t_yg)
            inc(SC, ss, lambda e, w2f=w2f: e.activation(w2f[:, :], w2f[:, :], AF.Ln))
            wait(SC, ss, ss.n)
            t_rec = inc(SC, ss, lambda e, w2f=w2f: e.activation(w2f[:, :], w2f[:, :],
                                                                AF.Exp, scale=-1.0))
            wait(VE, ss, t_rec)
            wait(VE, sd, t_y1)
            t_qwn = inc(VE, sv, lambda e, wf=wf, w2f=w2f: e.tensor_tensor(
                wf[:, :], wf[:, :], w2f[:, :], ALU.mult))
            wait(SY, sv, t_qwn)
            t_qr = dma(SY, sd, w2f[:, :], qry_d[t * 128:(t + 1) * 128, :])
            wait(VE, sd, t_qr)
            t_t2 = inc(VE, sv, lambda e, wf=wf, w2f=w2f, wbf=wbf: e.tensor_tensor(
                wbf[:, :], wf[:, :], w2f[:, :], ALU.mult))
            if t + 1 < NTB:
                b3_load(t + 1)
            cs_s, cs_v = cumsum_tile(wbf, wf, 1, t_t2, t == 0)
            wait(SY, ss, cs_s)
            t_w = dma(SY, sd, y2_d[t * 128:(t + 1) * 128, :], wf[:, :])
            pmark(par)
        t_c2 = carry_round(1, cs_v) if phases >= 5 else 0

        t_wl = load_wbig(WkaT, wv_free) if phases >= 5 else None
        pass_fence()

        def b4_load(t):
            par = (t % 2) if KPAR else 0
            pwaits(par)
            a = dma(SY, sd, wf2[par][:, :], y2_d[t * 128:(t + 1) * 128, :])
            b = dma(SY, sd, w2f2[par][:, :], key_d[t * 128:(t + 1) * 128, :])
            ldt[t] = (a, b)
        if NTB:
            b4_load(0)
        cs_s = cs_v = 0
        for t in range(NTB):
            par = (t % 2) if KPAR else 0
            wf, w2f, wbf, hf = wf2[par], w2f2[par], wbf2[par], hfm2[par]
            t_y2, t_k = ldt[t]
            t_pq = add_carry(wf, 1, t_c2, t_k)
            wait(VE, sd, t_k)
            wait(VE, sv, t_pq)
            t_mx = inc(VE, sv, lambda e, wf=wf, w2f=w2f: e.tensor_tensor(
                wf[:, :], wf[:, :], w2f[:, :], ALU.mult))
            wait(SY, sv, t_mx)
            t_mw = dma(SY, sd, mix_d[t * 128:(t + 1) * 128, :], wf[:, :])
            wait(VE, sv, t_mx)
            t_mb = inc(VE, sv, lambda e, wf=wf, wbf=wbf: e.tensor_copy(wbf[:, :], wf[:, :]))
            t_mfm_v, t_mfm_s = transpose_tile(wbf, lambda ch, hf=hf: hf[:, ch * 128:(ch + 1) * 128],
                                              t_mb, alt=bool(KB16 & 32))
            b7tok["v"], b7tok["s"] = t_mfm_v, t_mfm_s
            wait(SC, sd, t_mw)
            lastw = 0
            for sl in range(c.NSL):
                bank = sl % 4
                if t_wl is not None:
                    wait(TE, sg, t_wl[sl])
                wait(TE, sv, t_mfm_v)
                wait(TE, ss, t_mfm_s)
                wait(TE, sv, pbank_v[bank])
                wait(TE, ss, pbank_s[bank])
                for ch in range(c.CH):
                    inc(TE, st, lambda e, sl=sl, ch=ch, bank=bank, hf=hf: e.matmul(
                        pb[bank][:, 0:c.SW], hf[:, ch * 128:(ch + 1) * 128],
                        wbig[:, ch * c.D + sl * c.SW: ch * c.D + sl * c.SW + c.SW],
                        start=(ch == 0), stop=False))
                inc(TE, st, lambda e, sl=sl, bank=bank: e.matmul(
                    pb[bank][:, 0:c.SW], ones1_b[:, :],
                    bka_rb[0:1, sl * c.SW:(sl + 1) * c.SW],
                    start=False, stop=True))
                wait(SC, st, st.n)
                lastw = inc(SC, ss, lambda e, sl=sl, bank=bank, wf=wf: e.activation(
                    wf[:, sl * c.SW:(sl + 1) * c.SW], pb[bank][:, 0:c.SW],
                    AF.Exp, scale=c.QASCL))
                pbank_s[bank] = lastw
            if t + 1 < NTB:
                b4_load(t + 1)
            wait(SY, ss, lastw)
            t_kw = dma(SY, sd, kw_d[t * 128:(t + 1) * 128, :], wf[:, :])
            wait(VE, ss, lastw)
            t_kwb = inc(VE, sv, lambda e, wf=wf, wbf=wbf: e.tensor_copy(wbf[:, :], wf[:, :]))
            cs_s, cs_v = cumsum_tile(wbf, w2f, 2, t_kwb, t == 0)
            wait(SY, ss, cs_s)
            t_y3 = dma(SY, sd, y3_d[t * 128:(t + 1) * 128, :], w2f[:, :])
            pmark(par)
        t_c3 = carry_round(2, cs_v) if phases >= 5 else 0

        pass_fence()

        def b5a_load(t):
            par = (t % 2) if KPAR else 0
            pwaits(par)
            a = dma(SY, sd, wf2[par][:, :], kw_d[t * 128:(t + 1) * 128, :])
            b = dma(SY, sd, w2f2[par][:, :], y3_d[t * 128:(t + 1) * 128, :])
            ldt[t] = (a, b)
        if NTB:
            b5a_load(0)
        cs_s = cs_v = 0
        for t in range(NTB):
            par = (t % 2) if KPAR else 0
            wf, w2f, wbf = wf2[par], w2f2[par], wbf2[par]
            t_kw, t_y3 = ldt[t]
            t_yg = add_carry(w2f, 2, t_c3, t_y3)
            wait(SC, sv, t_yg)
            inc(SC, ss, lambda e, w2f=w2f: e.activation(w2f[:, :], w2f[:, :], AF.Ln))
            wait(SC, ss, ss.n)
            t_rec = inc(SC, ss, lambda e, w2f=w2f: e.activation(w2f[:, :], w2f[:, :],
                                                                AF.Exp, scale=-1.0))
            wait(VE, ss, t_rec)
            wait(VE, sd, t_y3)
            t_kwn = inc(VE, sv, lambda e, wf=wf, w2f=w2f: e.tensor_tensor(
                wf[:, :], wf[:, :], w2f[:, :], ALU.mult))
            wait(SY, sv, t_kwn)
            t_mx = dma(SY, sd, w2f[:, :], mix_d[t * 128:(t + 1) * 128, :])
            wait(VE, sd, t_mx)
            t_t4 = inc(VE, sv, lambda e, wf=wf, w2f=w2f, wbf=wbf: e.tensor_tensor(
                wbf[:, :], wf[:, :], w2f[:, :], ALU.mult))
            if t + 1 < NTB:
                b5a_load(t + 1)
            cs_s, cs_v = cumsum_tile(wbf, wf, 3, t_t4, t == 0)
            wait(SY, ss, cs_s)
            t_w = dma(SY, sd, y4_d[t * 128:(t + 1) * 128, :], wf[:, :])
            pmark(par)
        t_c4 = carry_round(3, cs_v) if phases >= 5 else 0

        pass_fence()

        def b5b_load(t):
            par = (t % 2) if KPAR else 0
            pwaits(par)
            a = dma(SY, sd, wf2[par][:, :], y4_d[t * 128:(t + 1) * 128, :])
            b = dma(SY, sd, w2f2[par][:, :], val_d[t * 128:(t + 1) * 128, :])
            ldt[t] = (a, b)
        if NTB:
            b5b_load(0)
        for t in range(NTB):
            par = (t % 2) if KPAR else 0
            wf, w2f, wbf = wf2[par], w2f2[par], wbf2[par]
            t_y4, t_v = ldt[t]
            t_pk = add_carry(wf, 3, t_c4, t_v)
            wait(VE, sd, t_v)
            wait(VE, sv, t_pk)
            t_ob = inc(VE, sv, lambda e, wf=wf, w2f=w2f: e.tensor_tensor(
                wf[:, :], wf[:, :], w2f[:, :], ALU.mult))
            wait(SY, sv, t_ob)
            t_xs = dma(SY, sd, w2f[:, :], xs2_d[t * 128:(t + 1) * 128, :])
            wait(VE, sd, t_xs)
            t_x3 = inc(VE, sv, lambda e, wf=wf, w2f=w2f: e.tensor_tensor(
                wf[:, :], w2f[:, :], wf[:, :], ALU.add))
            wait(SY, sv, t_x3)
            t_w3 = dma(SY, sd, xs3_d[t * 128:(t + 1) * 128, :], wf[:, :])
            wait(SC, sv, t_x3)
            if t + 1 < NTB:
                b5b_load(t + 1)
            t_l4 = ln_tile(wf[:, :], wbf[:, :], None, wbf[:, :])
            t_l4v, t_l4s = transpose_tile(wbf, lambda ch, t=t: lnx_reg[ch][:, t * 128:(t + 1) * 128],
                                          t_l4, alt=bool(KB16 & 64))
            b7tok["v"], b7tok["s"] = t_l4v, t_l4s
            wait(SY, sd, t_w3)
            pmark(par)
        b_v = sv.n
        b_s = ss.n
        A.pop()


        if phases >= 6:
            ffn_phase(b_v, b_s, xs3_d, y_out)

        SY.raw(lambda e: e.wait_ge(sd.h, sd.n))
        SY.raw(lambda e: e.wait_ge(sg.h, sg.n))

        @block.sync
        def _(eng):
            for f in SY.ops:
                f(eng)

        @block.gpsimd
        def _(eng):
            for f in GP.ops:
                f(eng)

        @block.tensor
        def _(eng):
            for f in TE.ops:
                f(eng)

        @block.vector
        def _(eng):
            for f in VE.ops:
                f(eng)

        @block.scalar
        def _(eng):
            for f in SC.ops:
                f(eng)

    return nc


def make_masks():
    f32 = np.float32
    ar = np.arange
    maske = ((ar(128)[:, None] + (ar(2 * QG)[None, :] // QG) * 128)
             <= (ar(2 * QG)[None, :] % QG)).astype(f32)
    masko = ((ar(128)[:, None] + (ar(4 * QG)[None, :] // QG) * 128)
             <= (QG + ar(4 * QG)[None, :] % QG)).astype(f32)
    return maske, masko


def make_in_maps(cfg, inp):
    c = cfg
    f32 = np.float32
    x = np.asarray(inp["L0_attn_a_x"], f32).reshape(c.T, c.D)
    wqT = np.ascontiguousarray(np.asarray(inp["L0_attn_a_wq"], f32).T)
    wkT = np.ascontiguousarray(np.asarray(inp["L0_attn_a_wk"], f32).T)
    wvT = np.ascontiguousarray(np.asarray(inp["L0_attn_a_wv"], f32).T)
    maske, masko = make_masks()
    brows = np.stack([
        np.asarray(inp["L0_attn_b_bq"], f32), np.asarray(inp["L0_attn_b_bk"], f32),
        np.asarray(inp["L0_attn_b_bv"], f32), np.asarray(inp["L0_attn_b_bqa"], f32),
        np.asarray(inp["L0_attn_b_bka"], f32)])
    gbrow = np.stack([np.asarray(inp["L0_attn_b_ln_g"], f32),
                      np.asarray(inp["L0_attn_b_ln_b"], f32)])
    common = {
        "x_full": x,
        "tau_in": np.asarray(inp["L0_attn_a_tau"], f32).reshape(1, 1),
        "nu_in": np.asarray(inp["L0_attn_a_nu"], f32).reshape(1, 1),
        "WqT": np.ascontiguousarray(np.asarray(inp["L0_attn_b_Wq"], f32).T),
        "WkT": np.ascontiguousarray(np.asarray(inp["L0_attn_b_Wk"], f32).T),
        "WvT": np.ascontiguousarray(np.asarray(inp["L0_attn_b_Wv"], f32).T),
        "WqaT": np.ascontiguousarray(np.asarray(inp["L0_attn_b_Wqa"], f32).T),
        "WkaT": np.ascontiguousarray(np.asarray(inp["L0_attn_b_Wka"], f32).T),
        "brows": brows, "gbrow": gbrow,
        "W_g": np.asarray(inp["L0_ffn_W_g"], f32),
        "We_gate": np.asarray(inp["L0_ffn_We_gate"], f32),
        "We_up": np.asarray(inp["L0_ffn_We_up"], f32),
        "We_down": np.asarray(inp["L0_ffn_We_down"], f32),
        "Ws_gate": np.asarray(inp["L0_ffn_Ws_gate"], f32),
        "Ws_up": np.asarray(inp["L0_ffn_Ws_up"], f32),
        "Ws_down": np.asarray(inp["L0_ffn_Ws_down"], f32),
        "tri_in": np.triu(np.ones((128, 128), f32)),
        "ident_in": np.eye(128, dtype=f32),
        "maske_in": maske, "masko_in": masko,
    }
    maps = []
    for cc in range(NC_):
        d = dict(common)
        d["x_loc"] = np.ascontiguousarray(x[cc * c.TL:(cc + 1) * c.TL])
        h0 = cc * HPC
        d["wqh"] = np.ascontiguousarray(wqT[:, h0 * c.DH:(h0 + HPC) * c.DH])
        d["wkh"] = np.ascontiguousarray(wkT[:, h0 * c.DH:(h0 + HPC) * c.DH])
        d["wvh"] = np.ascontiguousarray(wvT[:, h0 * c.DH:(h0 + HPC) * c.DH])
        pv = np.zeros((NC_, 1), f32)
        pv[:cc] = 1.0
        d["prefix_in"] = pv
        gi = np.zeros((128, c.NT * NC_), np.int32)
        for t in range(c.NT):
            for r in range(NC_):
                gi[:, t * NC_ + r] = r * c.T + cc * c.TL + t * 128 + np.arange(128)
        d["gidx_in"] = gi
        maps.append(d)
    return maps


def kernel(**inputs):
    cfg = REAL
    nc = build(cfg)
    in_maps = make_in_maps(cfg, inputs)
    res = run_bass_kernel_spmd(nc, in_maps, core_ids=list(range(NC_)))
    out = np.concatenate([np.asarray(res.results[cc]["y"]) for cc in range(NC_)], axis=0)
    return out.reshape(1, cfg.T, cfg.D).astype(np.float32)


# version 29 (final)
# speedup vs baseline: 1.0009x; 1.0009x over previous
import numpy as np
import ml_dtypes
from contextlib import ExitStack

import concourse.bass as bass
import concourse.mybir as mybir
from concourse.bass_core_rust import IndirectOffsetOnAxis
from concourse.bass_utils import run_bass_kernel_spmd

F32 = mybir.dt.float32
BF16 = mybir.dt.bfloat16
I32 = mybir.dt.int32
AF = mybir.ActivationFunctionType
ALU = mybir.AluOpType
AX = mybir.AxisListType.X

NC_ = 8
H = 16
HPC = 2
QG = 256
KBW = 512


class Cfg:
    def __init__(self, T, D, DE):
        self.T, self.D, self.DE = T, D, DE
        self.TL = T // NC_
        self.NT = self.TL // 128
        self.DH = D // H
        self.CH = D // 128
        self.VW = self.DH + 2
        self.NSL = max(1, D // 512)
        self.SW = min(512, D)
        self.NJE = DE // 128
        self.NEXP = 8
        self.NVE = self.NEXP + 2
        self.LNEPS = 1e-5
        self.L2EPS = 1e-8
        self.QASCL = 1.0 / float(np.sqrt(np.float32(D)))


REAL = Cfg(T=8192, D=2048, DE=1024)
TINY = Cfg(T=1024, D=512, DE=256)


class Eng:
    def __init__(self, name):
        self.name, self.ops = name, []

    def raw(self, fn):
        self.ops.append(fn)


class Sem:
    def __init__(self, name):
        self.name, self.h, self.n = name, None, 0


def inc(eng, sem, fn, k=1):
    sem.n += k
    def g(e, fn=fn, k=k, sem=sem):
        fn(e).then_inc(sem.h, k)
    eng.ops.append(g)
    return sem.n


def wait(eng, sem, tgt):
    if tgt is None or tgt <= 0:
        return
    eng.ops.append(lambda e, s=sem, t=tgt: e.wait_ge(s.h, t))


class Arena:
    def __init__(self, t, words):
        self.t, self.words, self.off, self.marks = t, words, 0, []

    def push(self):
        self.marks.append(self.off)

    def pop(self):
        self.off = self.marks.pop()

    def tile(self, parts, n, dt):
        w = n if dt in (F32, I32) else (n + 1) // 2
        w = (w + 15) & ~15
        assert self.off + w <= self.words, (
            f"arena overflow: need {w}w at {self.off}, cap {self.words}")
        ap = self.t.ap()[0:parts, self.off:self.off + w]
        self.off += w
        if dt == F32:
            return ap[:, 0:n]
        if dt == I32:
            return ap.bitcast(I32)[:, 0:n]
        return ap.bitcast(BF16)[:, 0:n]


def build(cfg, phases=6):
    c = cfg
    nc = bass.Bass("TRN2", target_bir_lowering=False, debug=False)

    def din(name, shape, dt=F32):
        return nc.dram_tensor(name, list(shape), dt, kind="ExternalInput")

    x_full = din("x_full", [c.T, c.D])
    x_loc = din("x_loc", [c.TL, c.D])
    wqh = din("wqh", [c.D, HPC * c.DH])
    wkh = din("wkh", [c.D, HPC * c.DH])
    wvh = din("wvh", [c.D, HPC * c.DH])
    tau_in = din("tau_in", [1, 1])
    nu_in = din("nu_in", [1, 1])
    WqT = din("WqT", [c.D, c.D])
    WkT = din("WkT", [c.D, c.D])
    WvT = din("WvT", [c.D, c.D])
    WqaT = din("WqaT", [c.D, c.D])
    WkaT = din("WkaT", [c.D, c.D])
    brows = din("brows", [5, c.D])
    gbrow = din("gbrow", [2, c.D])
    W_g = din("W_g", [c.D, c.NEXP])
    We_gate = din("We_gate", [c.NEXP, c.D, c.DE])
    We_up = din("We_up", [c.NEXP, c.D, c.DE])
    We_down = din("We_down", [c.NEXP, c.DE, c.D])
    Ws_gate = din("Ws_gate", [c.D, 2 * c.DE])
    Ws_up = din("Ws_up", [c.D, 2 * c.DE])
    Ws_down = din("Ws_down", [2 * c.DE, c.D])
    tri_in = din("tri_in", [128, 128], F32)
    ident_in = din("ident_in", [128, 128], F32)
    maske_in = din("maske_in", [128, 2 * QG], F32)
    masko_in = din("masko_in", [128, 4 * QG], F32)
    prefix_in = din("prefix_in", [NC_, 1], F32)
    gidx_in = din("gidx_in", [128, c.NT * NC_], I32)

    y_out = nc.dram_tensor("y", [c.TL, c.D], F32, kind="ExternalOutput")

    def scr(name, shape, dt=F32):
        return nc.dram_tensor(name, list(shape), dt)

    xs1_d = scr("xs1_d", [c.TL, c.D])
    xs2_d = scr("xs2_d", [c.TL, c.D])
    xs3_d = scr("xs3_d", [c.TL, c.D])
    acc_d = scr("acc_d", [c.TL, c.D])
    hfm_d = scr("hfm_d", [c.NT * 128, c.D], BF16)
    qfm_d = scr("qfm_d", [c.NT * 128, c.D], BF16)
    qry_d = scr("qry_d", [c.TL, c.D])
    qw_d = scr("qw_d", [c.TL, c.D])
    y1_d = scr("y1_d", [c.TL, c.D])
    key_d = scr("key_d", [c.TL, c.D])
    val_d = scr("val_d", [c.TL, c.D])
    y2_d = scr("y2_d", [c.TL, c.D])
    mix_d = scr("mix_d", [c.TL, c.D])
    kw_d = scr("kw_d", [c.TL, c.D])
    y3_d = scr("y3_d", [c.TL, c.D])
    y4_d = scr("y4_d", [c.TL, c.D])
    agin_h = [scr(f"agin{h}", [c.T, c.DH], BF16) for h in range(HPC)]
    agout_h = [scr(f"agout{h}", [NC_ * c.T, c.DH], BF16) for h in range(HPC)]
    ccin = scr("ccin", [1, c.D])
    ccout = scr("ccout", [NC_, c.D])

    SY, GP, TE, VE, SC = (Eng(n) for n in ("sync", "gpsimd", "tensor", "vector", "scalar"))
    sd, sg, scc, st, sv, ss = (Sem(n) for n in ("sd", "sg", "scc", "st", "sv", "ss"))
    sdx = [Sem("sdx0"), Sem("sdx1")]
    sgG, sgU, sgD, sga, sgi = (Sem(n) for n in ("sgG", "sgU", "sgD", "sga", "sgi"))
    sgi1 = Sem("sgi1")
    scin = Sem("scin")
    scin2 = Sem("scin2")
    scar = Sem("scar")

    es = ExitStack()
    with es:
        for s in (sd, sg, scc, st, sv, ss, sdx[0], sdx[1], sgG, sgU, sgD, sga, sgi,
                  sgi1, scin, scin2, scar):
            s.h = es.enter_context(nc.semaphore(s.name))

        AW = 52224
        arena_t = es.enter_context(nc.sbuf_tensor("arena", [128, AW], F32))
        A = Arena(arena_t, AW)

        tri_bf = A.tile(128, 128, BF16)
        ident_bf = A.tile(128, 128, BF16)
        prefix_sb = A.tile(NC_, 1, F32)
        gidx_sb = A.tile(128, c.NT * NC_, I32)
        ones128_bf = A.tile(128, 1, BF16)
        ones1_f = A.tile(1, 128, F32)
        ones1_b = A.tile(1, 128, BF16)
        onescol_f = A.tile(128, 1, F32)
        eps_ln = A.tile(128, 1, F32)
        tau_sb = A.tile(1, 1, F32)
        nu_tmp = A.tile(1, 1, F32)
        tau_col = A.tile(128, 1, F32)
        nu_col = A.tile(128, 1, F32)
        negabstau = A.tile(128, 1, F32)
        w8 = [A.tile(128, c.NEXP, F32) for _ in range(c.NT)]
        stat = A.tile(128, 8, F32)

        pb = [es.enter_context(nc.psum_tensor(f"pb{i}", [128, 512], F32))
              for i in range(8)]

        def pbf(i):
            return pb[i].ap().bitcast(BF16)

        block = es.enter_context(nc.Block())

        def dma(eng, sem, out, in_, **kw):
            return inc(eng, sem,
                       lambda e, o=out, i=in_, kw=kw: e.dma_start(out=o, in_=i, **kw), 16)

        def wload(eng, sem, dst2d, src2d, inner):
            s3 = src2d.rearrange("(k p) d -> p k d", p=128)
            d3 = dst2d.rearrange("p (k d) -> p k d", d=inner)
            K = s3.shape[1]
            npcs = max(1, (K * 128) // 512)
            npcs = min(npcs, K)
            step = (K + npcs - 1) // npcs
            t = None
            for k0 in range(0, K, step):
                k1 = min(K, k0 + step)
                t = dma(eng, sem, d3[:, k0:k1], s3[:, k0:k1])
            return t

        dma(GP, sg, tri_bf[:, :], tri_in[:, :])
        dma(GP, sg, ident_bf[:, :], ident_in[:, :])
        t_pfx = dma(SY, sd, prefix_sb[:, :], prefix_in[:, :])
        t_gidx = dma(SY, sd, gidx_sb[:, :], gidx_in[:, :])
        dma(SY, sd, tau_sb[:, :], tau_in[:, :])
        t_ld = dma(SY, sd, nu_tmp[:, :], nu_in[:, :])

        VE.raw(lambda e: e.memset(ones128_bf[:, :], 1.0))
        VE.raw(lambda e: e.memset(ones1_f[:, :], 1.0))
        VE.raw(lambda e: e.memset(ones1_b[:, :], 1.0))
        VE.raw(lambda e: e.memset(onescol_f[:, :], 1.0))
        inc(VE, sv, lambda e: e.memset(eps_ln[:, :], c.LNEPS))

        wait(TE, sd, t_ld)
        wait(TE, sv, sv.n)
        inc(TE, st, lambda e: e.matmul(pb[0][:, 0:1], ones1_f[:, :], tau_sb[:, :],
                                       start=True, stop=True))
        inc(TE, st, lambda e: e.matmul(pb[1][:, 0:1], ones1_f[:, :], nu_tmp[:, :],
                                       start=True, stop=True))
        t_te0 = st.n
        wait(VE, st, t_te0)
        inc(VE, sv, lambda e: e.tensor_copy(tau_col[:, :], pb[0][:, 0:1]))
        t_nu = inc(VE, sv, lambda e: e.tensor_copy(nu_col[:, :], pb[1][:, 0:1]))
        wait(SC, sv, t_nu)
        t_abs = inc(SC, ss, lambda e: e.activation(negabstau[:, :], tau_col[:, :], AF.Abs))
        wait(VE, ss, t_abs)
        inc(VE, sv, lambda e: e.tensor_scalar_mul(negabstau[:, :], negabstau[:, :], -1.0))

        def ln_tile(xap, out_bf_ap, out_f32_ap, dump_bf):
            wait(SC, sv, sv.n)
            wait(SC, ss, ss.n)
            wait(VE, ss, ss.n)
            wait(VE, sv, sv.n)
            t0 = inc(VE, sv, lambda e: e.reduce_sum(stat[:, 0:1], xap, axis=AX))
            t1 = inc(SC, ss, lambda e: e.activation(dump_bf, xap, AF.Square,
                                                    accum_out=stat[:, 1:2]))
            wait(VE, ss, t1)
            wait(VE, sv, t0)
            inc(VE, sv, lambda e: e.tensor_scalar_mul(stat[:, 2:3], stat[:, 0:1], -1.0 / c.D))
            wait(VE, sv, sv.n)
            inc(VE, sv, lambda e: e.tensor_scalar_mul(stat[:, 1:2], stat[:, 1:2], 1.0 / c.D))
            wait(VE, sv, sv.n)
            inc(VE, sv, lambda e: e.tensor_tensor(stat[:, 3:4], stat[:, 2:3], stat[:, 2:3], ALU.mult))
            wait(VE, sv, sv.n)
            t2 = inc(VE, sv, lambda e: e.tensor_tensor(stat[:, 1:2], stat[:, 1:2], stat[:, 3:4],
                                                       ALU.subtract))
            wait(SC, sv, t2)
            inc(SC, ss, lambda e: e.activation(stat[:, 3:4], stat[:, 1:2], AF.Ln, bias=eps_ln[:, :]))
            wait(SC, ss, ss.n)
            t3 = inc(SC, ss, lambda e: e.activation(stat[:, 3:4], stat[:, 3:4], AF.Exp, scale=-0.5))
            wait(VE, ss, t3)
            wait(VE, sv, t2)
            tv = None
            if out_bf_ap is not None:
                tv = inc(VE, sv, lambda e: e.tensor_scalar(
                    out_bf_ap, xap, stat[:, 2:3], stat[:, 3:4], ALU.add, ALU.mult))
            if out_f32_ap is not None:
                tv = inc(VE, sv, lambda e: e.tensor_scalar(
                    out_f32_ap, xap, stat[:, 2:3], stat[:, 3:4], ALU.add, ALU.mult))
            return tv

        def transpose_tile(src_bf, dst_fn, t_src_v, alt=True):
            wait(TE, sv, t_src_v)
            evs = [(None, 0)] * c.CH
            for ch in range(c.CH):
                bank = 6 + (ch % 2)
                if ch >= 2:
                    sem_, tgt_ = evs[ch - 2]
                    wait(TE, sem_, tgt_)
                inc(TE, st, lambda e, ch=ch, bank=bank: e.transpose(
                    pbf(bank)[:, 0:128], src_bf[:, ch * 128:(ch + 1) * 128], ident_bf[:, :]))
                if ch % 2 == 0 or not alt:
                    wait(VE, st, st.n)
                    evs[ch] = (sv, inc(VE, sv, lambda e, ch=ch, bank=bank: e.tensor_copy(
                        dst_fn(ch), pbf(bank)[:, 0:128])))
                else:
                    wait(SC, st, st.n)
                    evs[ch] = (ss, inc(SC, ss, lambda e, ch=ch, bank=bank: e.copy(
                        dst_fn(ch), pbf(bank)[:, 0:128])))
            sv_t = max((t for s_, t in evs if s_ is sv), default=0)
            ss_t = max((t for s_, t in evs if s_ is ss), default=0)
            return sv_t, ss_t

        A.push()
        maske_bf = A.tile(128, 2 * QG, BF16)
        masko_bf = A.tile(128, 4 * QG, BF16)
        dma(GP, sg, maske_bf[:, :], maske_in[:, :])
        dma(GP, sg, masko_bf[:, :], masko_in[:, :])
        Qb = [A.tile(c.DH, c.T, BF16) for _ in range(HPC)]
        Kb = [A.tile(c.DH, c.T, BF16) for _ in range(HPC)]
        Vb = [A.tile(128, (c.T // 128) * c.VW, BF16) for _ in range(HPC)]
        A.push()
        xa_t = [A.tile(128, c.D, F32) for _ in range(2)]
        lnx_bf2 = [A.tile(128, c.D, BF16) for _ in range(2)]
        lnx_fm2 = [A.tile(128, c.CH * 128, BF16) for _ in range(2)]
        qn_bf2 = [A.tile(128, 6 * c.DH, BF16) for _ in range(2)]
        fcol2 = [A.tile(128, 8, F32) for _ in range(2)]
        dmp_sq = A.tile(128, c.DH, BF16)
        wq_sb = A.tile(128, c.CH * HPC * c.DH, BF16)
        wk_sb = A.tile(128, c.CH * HPC * c.DH, BF16)
        wv_sb = A.tile(128, c.CH * HPC * c.DH, BF16)

        for h in range(HPC):
            inc(VE, sv, lambda e, h=h: e.memset(
                Vb[h].rearrange("p (s w) -> p s w", w=c.VW)[:, :, c.DH:c.DH + 1], 1.0))
        v_ones = sv.n

        t_wq = wload(GP, sg, wq_sb[:, :], wqh.ap(), HPC * c.DH)
        t_wk = wload(GP, sg, wk_sb[:, :], wkh.ap(), HPC * c.DH)
        t_wv = wload(GP, sg, wv_sb[:, :], wvh.ap(), HPC * c.DH)

        NTA = c.T // 128
        x_free = [0, 0]
        lnx_free_t = [0, 0]
        lnb_free = [0, 0]
        bank_free = [0] * 6
        fac_prev = [0, 0]
        wait(TE, sg, t_wv)
        ph_list = [(wq_sb, 0, 'q'), (wq_sb, 1, 'q'), (wk_sb, 0, 'k'),
                   (wk_sb, 1, 'k'), (wv_sb, 0, 'v'), (wv_sb, 1, 'v')]
        for ti in range(NTA):
            xi = ti % 2
            lnx_bf = lnx_bf2[xi]
            lnx_fm = lnx_fm2[xi]
            qn_bf = qn_bf2[xi]
            fcol = fcol2[xi]
            wait(SY, sv, x_free[xi])
            t_x = dma(SY, sdx[xi], xa_t[xi][:, :], x_full[ti * 128:(ti + 1) * 128, :])
            wait(VE, sdx[xi], t_x)
            wait(SC, sdx[xi], t_x)
            wait(VE, st, lnb_free[xi])
            t_ln = ln_tile(xa_t[xi][:, :], lnx_bf[:, :], None, lnx_bf[:, :])
            x_free[xi] = t_ln
            wait(TE, st, lnx_free_t[xi])
            t_fm_v, t_fm_s = transpose_tile(
                lnx_bf, lambda ch, dst=lnx_fm: dst[:, ch * 128:(ch + 1) * 128], t_ln)
            lnb_free[xi] = st.n
            wait(TE, sv, t_fm_v)
            wait(TE, ss, t_fm_s)
            if ti == 0:
                wait(TE, sv, v_ones)
            for phi, (wsb, hh, kind) in enumerate(ph_list):
                bank = phi
                wait(TE, ss, bank_free[phi])
                for ch in range(c.CH):
                    inc(TE, st, lambda e, wsb=wsb, hh=hh, ch=ch, bank=bank,
                        lnx_fm=lnx_fm: e.matmul(
                        pb[bank][:, 0:c.DH],
                        lnx_fm[:, ch * 128:(ch + 1) * 128],
                        wsb[:, ch * HPC * c.DH + hh * c.DH: ch * HPC * c.DH + (hh + 1) * c.DH],
                        start=(ch == 0), stop=(ch == c.CH - 1)))
                wait(SC, st, st.n)
                if phi == 0:
                    wait(SC, sv, fac_prev[xi])
                bank_free[phi] = inc(SC, ss, lambda e, bank=bank, phi=phi, qn_bf=qn_bf: e.copy(
                    qn_bf[:, phi * c.DH:(phi + 1) * c.DH], pb[bank][:, 0:c.DH]))
                wait(SC, ss, ss.n)
                inc(SC, ss, lambda e, phi=phi, qn_bf=qn_bf, fcol=fcol: e.activation(
                    dmp_sq[:, 0:c.DH], qn_bf[:, phi * c.DH:(phi + 1) * c.DH], AF.Square,
                    accum_out=fcol[:, phi:phi + 1]))
            sq_all = ss.n
            lnx_free_t[xi] = st.n
            wait(SC, ss, sq_all)
            inc(SC, ss, lambda e, fcol=fcol: e.activation(fcol[:, 0:6], fcol[:, 0:6], AF.Ln))
            wait(SC, ss, ss.n)
            t_f = inc(SC, ss, lambda e, fcol=fcol: e.activation(fcol[:, 0:6], fcol[:, 0:6],
                                                                AF.Exp, scale=0.5))
            wait(VE, ss, t_f)
            inc(VE, sv, lambda e, fcol=fcol: e.tensor_scalar_add(fcol[:, 0:6], fcol[:, 0:6], c.L2EPS))
            wait(VE, sv, sv.n)
            inc(VE, sv, lambda e, fcol=fcol: e.reciprocal(fcol[:, 0:6], fcol[:, 0:6]))
            wait(VE, sv, sv.n)
            inc(VE, sv, lambda e, fcol=fcol: e.tensor_scalar(fcol[:, 0:1], fcol[:, 0:1],
                                                             tau_col[:, 0:1], None, ALU.mult))
            wait(VE, sv, sv.n)
            t_fac = inc(VE, sv, lambda e, fcol=fcol: e.tensor_scalar(fcol[:, 1:2], fcol[:, 1:2],
                                                                     tau_col[:, 0:1], None, ALU.mult))
            wait(VE, sv, t_fac)
            tr_free = [0, 0]
            for phi, (wsb, hh, kind) in enumerate(ph_list):
                qs = qn_bf[:, phi * c.DH:(phi + 1) * c.DH]
                if kind == 'v':
                    dst = Vb[hh].rearrange("p (s w) -> p s w", w=c.VW)[:, ti, 0:c.DH]
                    inc(VE, sv, lambda e, qs=qs, phi=phi, dst=dst, fcol=fcol: e.tensor_scalar(
                        dst, qs, fcol[:, phi:phi + 1], None, ALU.mult))
                else:
                    t_n = inc(VE, sv, lambda e, qs=qs, phi=phi, fcol=fcol: e.tensor_scalar(
                        qs, qs, fcol[:, phi:phi + 1], None, ALU.mult))
                    tb = 6 + (phi % 2)
                    wait(TE, sv, t_n)
                    wait(TE, sv, tr_free[phi % 2])
                    inc(TE, st, lambda e, tb=tb, qs=qs: e.transpose(
                        pbf(tb)[0:c.DH, 0:128], qs, ident_bf[:, :]))
                    wait(VE, st, st.n)
                    dstfm = (Qb if kind == 'q' else Kb)[hh][0:c.DH, ti * 128:(ti + 1) * 128]
                    tr_free[phi % 2] = inc(VE, sv, lambda e, tb=tb, dstfm=dstfm: e.tensor_copy(
                        dstfm, pbf(tb)[0:c.DH, 0:128]))
            fac_prev[xi] = sv.n
        a2_v = sv.n
        A.pop()
        A.push()
        contrib = A.tile(128, (c.T // 128) * HPC * c.DH, BF16)
        p_bf = [A.tile(128, 4 * QG, BF16) for _ in range(2)]
        p_bf2_extra = A.tile(128, 4 * QG, BF16)
        l_col = A.tile(128, 2, F32)

        NQG = c.T // QG
        NPS = 3
        prev_pv = [0] * NPS
        prev_exp = [0] * NPS
        o_free = 0
        wait(TE, sv, a2_v)
        p_bf3 = p_bf + [p_bf2_extra]
        for hh in range(HPC if phases >= 2 else 0):
            for g in range(NQG):
                nfull = (g * QG) // KBW
                kblocks = [(i * KBW, 4, False) for i in range(nfull)]
                kblocks.append((nfull * KBW, 2 if g % 2 == 0 else 4, True))
                nkb = len(kblocks)
                def emit_pv(pend_):
                    i_, par_, chunks_, k0_, dsem_, dtgt_ = pend_
                    wait(TE, dsem_, dtgt_)
                    for qt in range(2):
                        for ci in range(chunks_):
                            slot = k0_ // 128 + ci
                            first = (i_ == 0 and ci == 0)
                            last = (i_ == nkb - 1 and ci == chunks_ - 1)
                            inc(TE, st, lambda e, hh=hh, qt=qt, ci=ci, par=par_, slot=slot,
                                first=first, last=last: e.matmul(
                                pb[6 + qt][:, 0:c.DH + 1],
                                p_bf3[par][:, ci * QG + qt * 128: ci * QG + qt * 128 + 128],
                                Vb[hh].rearrange("p (s w) -> p s w", w=c.VW)[:, slot, 0:c.DH + 1],
                                start=first, stop=last))
                    prev_pv[par_] = st.n

                pend = None
                for i, (k0, chunks, diag) in enumerate(kblocks):
                    par = i % NPS
                    pk0 = 2 * par
                    wait(TE, ss, prev_exp[par])
                    if i == 0:
                        wait(TE, sv, o_free)
                    for ci in range(chunks):
                        inc(TE, st, lambda e, hh=hh, g=g, k0=k0, ci=ci, pk0=pk0: e.matmul(
                            pb[pk0 + ci // 2][:, (ci % 2) * QG:(ci % 2 + 1) * QG],
                            Kb[hh][0:c.DH, k0 + ci * 128:k0 + (ci + 1) * 128],
                            Qb[hh][0:c.DH, g * QG:(g + 1) * QG],
                            start=True, stop=True))
                    t_sT = st.n
                    wait(SC, st, t_sT)
                    wait(SC, st, prev_pv[par])
                    for b in range((chunks + 1) // 2):
                        w_ = min(2 * QG, (chunks - 2 * b) * QG)
                        inc(SC, ss, lambda e, b=b, w_=w_, par=par, pk0=pk0: e.activation(
                            p_bf3[par][:, b * 2 * QG: b * 2 * QG + w_],
                            pb[pk0 + b][:, 0:w_], AF.Exp, bias=negabstau[:, :]))
                    prev_exp[par] = ss.n
                    dep_sem, dep_tgt = ss, ss.n
                    if diag:
                        wait(VE, ss, ss.n)
                        m = maske_bf if g % 2 == 0 else masko_bf
                        dep_tgt = inc(VE, sv, lambda e, m=m, par=par, chunks=chunks: e.tensor_tensor(
                            p_bf3[par][:, 0:chunks * QG], p_bf3[par][:, 0:chunks * QG],
                            m[:, 0:chunks * QG], ALU.mult))
                        dep_sem = sv
                    if pend is not None:
                        emit_pv(pend)
                    pend = (i, par, chunks, k0, dep_sem, dep_tgt)
                emit_pv(pend)
                wait(VE, st, st.n)
                for qt in range(2):
                    inc(VE, sv, lambda e, qt=qt: e.reciprocal(
                        l_col[:, qt:qt + 1], pb[6 + qt][:, c.DH:c.DH + 1]))
                    wait(VE, sv, sv.n)
                    inc(VE, sv, lambda e, qt=qt: e.tensor_tensor(
                        l_col[:, qt:qt + 1], l_col[:, qt:qt + 1], nu_col[:, :], ALU.mult))
                    wait(VE, sv, sv.n)
                    gt = g * 2 + qt
                    dst = contrib[:, gt * HPC * c.DH + hh * c.DH: gt * HPC * c.DH + (hh + 1) * c.DH]
                    inc(VE, sv, lambda e, qt=qt, dst=dst: e.tensor_scalar(
                        dst, pb[6 + qt][:, 0:c.DH], l_col[:, qt:qt + 1], None, ALU.mult))
                o_free = sv.n
            wait(SY, sv, sv.n)
            _ag3 = agin_h[hh].ap().rearrange("(g p) f -> p g f", p=128)
            _ct4 = contrib.rearrange("p (g h f) -> p g h f", h=HPC, f=c.DH)
            NGW = c.T // 128
            spillsem = scin if hh == 0 else scin2
            t_cw = None
            for g0 in range(0, NGW, 8):
                g1 = min(NGW, g0 + 8)
                t_cw = dma(SY, spillsem, _ag3[:, g0:g1], _ct4[:, g0:g1, hh, :])
            wait(GP, spillsem, t_cw)
            t_ag1 = inc(GP, scc, lambda e, hh=hh: e.collective_compute(
                "AllGather", ALU.bypass, replica_groups=[list(range(NC_))],
                ins=[agin_h[hh].ap().opt()], outs=[agout_h[hh].ap().opt()]))
        if phases < 2:
            t_ag1 = 0
        A.pop()
        A.pop()

        lnx_reg = [A.tile(128, c.TL, BF16) for _ in range(c.CH)]

        A.push()
        att_t2 = [A.tile(128, HPC * c.DH * NC_, BF16) for _ in range(2)]
        xr_t2 = [A.tile(128, c.D, F32) for _ in range(2)]
        xs_t2 = [A.tile(128, c.D, F32) for _ in range(2)]
        lnt_bf2 = [A.tile(128, c.D, BF16) for _ in range(2)]

        wait(GP, scc, t_ag1)
        wait(GP, sd, t_ld)
        bfv = [0, 0]
        bfs = [0, 0]
        bfd = [0, 0]
        for t in range(c.NT if phases >= 3 else 0):
            par = t % 2
            att_t, xr_t = att_t2[par], xr_t2[par]
            xs_t, lnt_bf = xs_t2[par], lnt_bf2[par]
            wait(GP, sv, bfv[par])
            wait(GP, ss, bfs[par])
            gsem = sgi if par == 0 else sgi1
            for r in range(NC_):
                col = t * NC_ + r
                for h in range(HPC):
                    t_att = inc(GP, gsem, lambda e, r=r, col=col, h=h, att_t=att_t:
                                e.indirect_dma_start(
                        att_t[:, r * HPC * c.DH + h * c.DH:
                              r * HPC * c.DH + (h + 1) * c.DH], None,
                        agout_h[h].ap(),
                        IndirectOffsetOnAxis(gidx_sb[:, col:col + 1], axis=0)), 16)
            wait(SY, sv, bfv[par])
            wait(SY, ss, bfs[par])
            t_xl = dma(SY, sd, xr_t[:, :], x_loc[t * 128:(t + 1) * 128, :])
            wait(VE, gsem, t_att)
            wait(VE, sd, t_xl)
            wait(VE, sd, bfd[par])
            t_add = inc(VE, sv, lambda e, xs_t=xs_t, xr_t=xr_t, att_t=att_t:
                        e.tensor_tensor(xs_t[:, :], xr_t[:, :], att_t[:, :], ALU.add))
            wait(SY, sv, t_add)
            t_w1 = dma(SY, sd, xs1_d[t * 128:(t + 1) * 128, :], xs_t[:, :])
            wait(SC, sv, t_add)
            t_ln2 = ln_tile(xs_t[:, :], lnt_bf[:, :], None, lnt_bf[:, :])
            transpose_tile(lnt_bf, lambda ch, t=t: lnx_reg[ch][:, t * 128:(t + 1) * 128], t_ln2)
            bfv[par] = sv.n
            bfs[par] = ss.n
            bfd[par] = t_w1
        a4_v = sv.n
        a4_s = ss.n
        wait(SY, sd, sd.n)
        A.pop()

        def ffn_phase(lnx_dep_v, lnx_dep_s, xs_in_d, xs_out_d):
            A.push()
            wg_sb = A.tile(128, c.CH * c.NEXP, BF16)
            r8 = A.tile(128, c.NEXP, F32)
            rst = A.tile(128, 4, F32)
            slotG_off = A.off
            slotG = A.tile(128, c.CH * c.DE, BF16)
            slotU = A.tile(128, c.CH * c.DE, BF16)
            slotD = A.tile(128, c.NJE * c.D, BF16)
            acc = A.tile(128, c.NT * c.D, F32)
            TSW = min(512, c.TL)
            NTS = c.TL // TSW
            act_off = A.off
            act_bf = A.tile(128, c.NJE * 512, BF16)
            xsb = A.t.ap()[0:128, act_off:act_off + c.D]

            wait(GP, st, st.n)
            wait(GP, sv, sv.n)
            wait(GP, ss, ss.n)
            wait(GP, sd, sd.n)
            t_wg = wload(GP, sgG, wg_sb[:, :], W_g.ap(), c.NEXP)

            def vexp(e):
                if e < c.NEXP:
                    return (We_gate.ap()[e], We_up.ap()[e], We_down.ap()[e], 'r', e)
                hh = e - c.NEXP
                return (Ws_gate.ap()[:, hh * c.DE:(hh + 1) * c.DE],
                        Ws_up.ap()[:, hh * c.DE:(hh + 1) * c.DE],
                        Ws_down.ap()[hh * c.DE:(hh + 1) * c.DE, :], 's', hh)

            def load_slot(slot_sb, sem, src_ap, free_t, inner):
                wait(GP, st, free_t)
                return wload(GP, sem, slot_sb[:, :], src_ap, inner)

            slotG3 = slotG.rearrange("p (k d) -> p k d", d=c.DE)
            slotU3 = slotU.rearrange("p (k d) -> p k d", d=c.DE)

            def chunk_load(eng, slot3, sem, src2, jc, free_t):
                wait(eng, st, free_t)
                s3 = src2.rearrange("(k p) d -> p k d", p=128)
                return dma(eng, sem, slot3[:, :, jc * 128:(jc + 1) * 128],
                           s3[:, :, jc * 128:(jc + 1) * 128])

            tokG, tokU, tokD = {}, {}, {}
            g0, u0, d0, _, _ = vexp(0)
            t_g0 = load_slot(slotG, sgG, g0, 0, c.DE)
            t_u0 = load_slot(slotU, sgU, u0, 0, c.DE)
            tokD[0] = load_slot(slotD, sgD, d0, 0, c.D)
            for jc in range(c.NJE):
                tokG[(0, jc)] = t_g0
                tokU[(0, jc)] = t_u0
            wait(TE, sgG, t_g0)
            wait(TE, sv, lnx_dep_v)
            wait(TE, ss, lnx_dep_s)
            for t in range(c.NT):
                for ch in range(c.CH):
                    inc(TE, st, lambda e, t=t, ch=ch: e.matmul(
                        pb[6][:, 0:c.NEXP], lnx_reg[ch][:, t * 128:(t + 1) * 128],
                        wg_sb[:, ch * c.NEXP:(ch + 1) * c.NEXP],
                        start=(ch == 0), stop=(ch == c.CH - 1)))
                wait(VE, st, st.n)
                wait(VE, ss, ss.n)
                inc(VE, sv, lambda e: e.reduce_max(rst[:, 0:1], pb[6][:, 0:c.NEXP],
                                                   axis=AX, negate=True))
                wait(SC, sv, sv.n)
                t_e = inc(SC, ss, lambda e: e.activation(r8[:, :], pb[6][:, 0:c.NEXP],
                                                         AF.Exp, bias=rst[:, 0:1],
                                                         accum_out=rst[:, 1:2]))
                wait(VE, ss, t_e)
                inc(VE, sv, lambda e: e.reciprocal(rst[:, 2:3], rst[:, 1:2]))
                wait(VE, sv, sv.n)
                inc(VE, sv, lambda e: e.tensor_scalar(r8[:, :], r8[:, :], rst[:, 2:3], None, ALU.mult))
                wait(VE, sv, sv.n)
                inc(VE, sv, lambda e: e.reduce_max(rst[:, 0:1], r8[:, :], axis=AX))
                wait(VE, sv, sv.n)
                inc(VE, sv, lambda e, t=t: e.tensor_scalar(w8[t][:, :], r8[:, :], rst[:, 0:1],
                                                           None, ALU.is_ge))
                wait(VE, sv, sv.n)
                inc(VE, sv, lambda e, t=t: e.scalar_tensor_tensor(
                    w8[t][:, :], w8[t][:, :], -2.0, r8[:, :], ALU.mult, ALU.add))
                wait(VE, sv, sv.n)
                inc(VE, sv, lambda e, t=t: e.reduce_max(rst[:, 1:2], w8[t][:, :], axis=AX))
                wait(VE, sv, sv.n)
                inc(VE, sv, lambda e, t=t: e.tensor_scalar(w8[t][:, :], r8[:, :], rst[:, 1:2],
                                                           None, ALU.is_ge))
                wait(VE, sv, sv.n)
                t_w8 = inc(VE, sv, lambda e, t=t: e.tensor_tensor(w8[t][:, :], r8[:, :],
                                                                  w8[t][:, :], ALU.mult))
                wait(TE, sv, t_w8)
            router_v = sv.n

            act_par = [0, 0]
            dbank_tok = [0, 0]
            for e_i in range(c.NVE):
                gap_, uap_, dap_, wk_, wi_ = vexp(e_i)
                if e_i + 1 < c.NVE:
                    gn, un, _, _, _ = vexp(e_i + 1)
                for tsl in range(NTS):
                    t0 = tsl * TSW
                    act_v = 0
                    for jc in range(c.NJE):
                        gb, ub = (0, 1) if jc % 2 == 0 else (4, 5)
                        if tsl == 0:
                            wait(TE, sgG, tokG[(e_i, jc)])
                            wait(TE, sgU, tokU[(e_i, jc)])
                            if e_i == 0 and jc == 0:
                                wait(TE, sv, router_v)
                        if jc >= 2:
                            wait(TE, sv, act_par[jc % 2])
                        for ch in range(c.CH):
                            inc(TE, st, lambda e, jc=jc, ch=ch, t0=t0, gb=gb: e.matmul(
                                pb[gb][:, 0:TSW],
                                slotG[:, ch * c.DE + jc * 128: ch * c.DE + (jc + 1) * 128],
                                lnx_reg[ch][:, t0:t0 + TSW],
                                start=(ch == 0), stop=(ch == c.CH - 1)))
                        for ch in range(c.CH):
                            inc(TE, st, lambda e, jc=jc, ch=ch, t0=t0, ub=ub: e.matmul(
                                pb[ub][:, 0:TSW],
                                slotU[:, ch * c.DE + jc * 128: ch * c.DE + (jc + 1) * 128],
                                lnx_reg[ch][:, t0:t0 + TSW],
                                start=(ch == 0), stop=(ch == c.CH - 1)))
                        t_gu = st.n
                        if tsl == NTS - 1 and e_i + 1 < c.NVE:
                            tokG[(e_i + 1, jc)] = chunk_load(GP, slotG3, sgG, gn, jc, t_gu)
                            tokU[(e_i + 1, jc)] = chunk_load(GP, slotU3, sgU, un, jc, t_gu)
                        wait(SC, st, t_gu)
                        wait(SC, sv, act_par[jc % 2])
                        t_si = inc(SC, ss, lambda e, jc=jc, gb=gb: e.activation(
                            act_bf[:, jc * 512: jc * 512 + TSW], pb[gb][:, 0:TSW], AF.Silu))
                        wait(VE, ss, t_si)
                        act_v = inc(VE, sv, lambda e, jc=jc, ub=ub: e.tensor_tensor(
                            act_bf[:, jc * 512: jc * 512 + TSW],
                            act_bf[:, jc * 512: jc * 512 + TSW],
                            pb[ub][:, 0:TSW], ALU.mult))
                        act_par[jc % 2] = act_v
                    wait(TE, sgD, tokD[e_i])
                    wait(TE, sv, act_v)
                    for ts2 in range(TSW // 128):
                        tt = tsl * (TSW // 128) + ts2
                        for dsl in range(c.NSL):
                            bank = 2 + (dsl % 2)
                            wait(TE, sv, dbank_tok[dsl % 2])
                            for jc in range(c.NJE):
                                inc(TE, st, lambda e, ts2=ts2, dsl=dsl, jc=jc, bank=bank: e.matmul(
                                    pb[bank][:, 0:c.SW],
                                    act_bf[:, jc * 512 + ts2 * 128: jc * 512 + ts2 * 128 + 128],
                                    slotD[:, jc * c.D + dsl * c.SW: jc * c.D + (dsl + 1) * c.SW],
                                    start=(jc == 0), stop=(jc == c.NJE - 1)))
                            wait(VE, st, st.n)
                            wcol = w8[tt][:, wi_:wi_ + 1] if wk_ == 'r' else onescol_f[:, 0:1]
                            aslice = acc[:, tt * c.D + dsl * c.SW: tt * c.D + (dsl + 1) * c.SW]
                            if e_i == 0:
                                inc(VE, sv, lambda e, bank=bank, wcol=wcol, aslice=aslice:
                                    e.tensor_scalar(aslice, pb[bank][:, 0:c.SW],
                                                    wcol, None, ALU.mult))
                            else:
                                inc(VE, sv, lambda e, bank=bank, wcol=wcol, aslice=aslice:
                                    e.scalar_tensor_tensor(aslice, pb[bank][:, 0:c.SW],
                                                           wcol, aslice, ALU.mult, ALU.add))
                            dbank_tok[dsl % 2] = sv.n
                if e_i + 1 < c.NVE:
                    _, _, dn, _, _ = vexp(e_i + 1)
                    tokD[e_i + 1] = load_slot(slotD, sgD, dn, st.n, c.D)

            acc_v = sv.n
            xsb2 = [xsb, A.t.ap()[0:128, slotG_off:slotG_off + c.D]]
            prev_w = [0, 0]
            wait(SY, sv, acc_v)
            wait(SY, st, st.n)
            wait(SY, ss, ss.n)
            wait(SY, sd, sd.n)
            for t in range(c.NT):
                xb = xsb2[t % 2]
                wait(SY, sd, prev_w[t % 2])
                t_x = dma(SY, sd, xb, xs_in_d[t * 128:(t + 1) * 128, :])
                wait(VE, sd, t_x)
                t_s = inc(VE, sv, lambda e, t=t, xb=xb: e.tensor_tensor(
                    xb, xb, acc[:, t * c.D:(t + 1) * c.D], ALU.add))
                wait(SY, sv, t_s)
                prev_w[t % 2] = dma(SY, sd, xs_out_d[t * 128:(t + 1) * 128, :], xb)
            wait(SY, sd, sd.n)
            wait(VE, sd, sd.n)
            A.pop()

        if phases >= 4:
            ffn_phase(a4_v, a4_s, xs1_d, xs2_d)

        A.push()
        wbig = A.tile(128, c.CH * c.D, BF16)
        hfm2 = [A.tile(128, c.CH * 128, BF16) for _ in range(2)]
        wf2 = [A.tile(128, c.D, F32) for _ in range(2)]
        w2f2 = [A.tile(128, c.D, F32) for _ in range(2)]
        wbf2 = [A.tile(128, c.D, BF16) for _ in range(2)]
        carry_sb = A.tile(NC_, c.D, F32)
        gam_rep = A.tile(128, c.D, BF16)
        bet_rep = A.tile(128, c.D, BF16)
        brep = A.tile(128, c.D, BF16)
        tloc_row = A.tile(1, c.D, F32)
        carry_row = A.tile(1, c.D, F32)
        bqa_rb = A.tile(1, c.D, BF16)
        bka_rb = A.tile(1, c.D, BF16)
        tloc_rb = A.tile(1, c.D, BF16)
        carry_rb = A.tile(1, c.D, BF16)

        def load_wbig(src_, wfree=None):
            if wfree is None:
                wait(GP, st, st.n)
                wait(GP, ss, ss.n)
                wait(GP, sv, sv.n)
                wait(GP, sga, sga.n)
            s3 = src_.ap().rearrange("(k p) d -> p k d", p=128)
            d3 = wbig.rearrange("p (k d) -> p k d", d=c.D)
            toks = []
            for sl in range(c.NSL):
                if wfree is not None:
                    wait(GP, st, wfree[sl])
                toks.append(dma(GP, sg, d3[:, :, sl * c.SW:(sl + 1) * c.SW],
                                s3[:, :, sl * c.SW:(sl + 1) * c.SW]))
            return toks

        import os as _os
        KB16 = int(_os.environ.get("KB16", "5"))
        KPAR = int(_os.environ.get("KPAR", "1"))

        rowtok = {"tloc_free": 0}

        def crow_ap(i):
            return carry_row[:, :]

        def cloc_ap(i):
            return tloc_row[:, :]

        def fill_rep(rep, srcrow):
            wait(SY, st, st.n)
            wait(SY, sv, sv.n)
            wait(SY, ss, ss.n)
            wait(SY, sd, sd.n)
            wait(SY, scin, scin.n)
            t_row = dma(SY, sd, tloc_row[:, :], srcrow)
            wait(TE, sd, t_row)
            for k in range(c.NSL):
                s0 = k * c.SW
                wait(TE, sv, sv.n)
                inc(TE, st, lambda e, s0=s0: e.matmul(
                    pb[2][:, 0:c.SW], ones1_f[:, :], tloc_row[0:1, s0:s0 + c.SW],
                    start=True, stop=True))
                wait(VE, st, st.n)
                inc(VE, sv, lambda e, s0=s0, rep=rep: e.tensor_copy(
                    rep[:, s0:s0 + c.SW], pb[2][:, 0:c.SW]))

        NTB = c.NT if phases >= 5 else 0
        if phases >= 5:
            wait(SY, st, st.n)
            wait(SY, sv, sv.n)
            wait(SY, ss, ss.n)
            wait(SY, sga, sga.n)
            t_r1 = dma(SY, sd, tloc_row[:, :], brows[3:4, :])
            wait(VE, sd, t_r1)
            t_c1b = inc(VE, sv, lambda e: e.tensor_copy(bqa_rb[:, :], tloc_row[:, :]))
            wait(SY, sv, t_c1b)
            t_r2 = dma(SY, sd, tloc_row[:, :], brows[4:5, :])
            wait(VE, sd, t_r2)
            inc(VE, sv, lambda e: e.tensor_copy(bka_rb[:, :], tloc_row[:, :]))
            fill_rep(gam_rep, gbrow[0:1, :])
            fill_rep(bet_rep, gbrow[1:2, :])

        pbank_v = [0, 0, 0, 0]
        b7tok = {"v": 0, "s": 0}
        _B0_FENCE = True
        pbank_s = [0, 0, 0, 0]
        pdone = [dict(v=0, s=0, d=0), dict(v=0, s=0, d=0)]

        def pwaits(par):
            wait(SY, sv, pdone[par]["v"])
            wait(SY, ss, pdone[par]["s"])
            wait(SY, sd, pdone[par]["d"])

        def pmark(par):
            pdone[par]["v"] = sv.n
            pdone[par]["s"] = ss.n
            pdone[par]["d"] = sd.n

        def pass_fence():
            for par in (0, 1):
                pdone[par]["v"] = sv.n
                pdone[par]["s"] = ss.n
                pdone[par]["d"] = sd.n

        if phases >= 5:
            pass_fence()
        ldt = {}

        def b0_load(t):
            par = (t % 2) if KPAR else 0
            pwaits(par)
            ldt[t] = dma(SY, sd, wf2[par][:, :], xs2_d[t * 128:(t + 1) * 128, :])
        if NTB:
            b0_load(0)
        for t in range(NTB):
            par = (t % 2) if KPAR else 0
            wf, w2f, wbf, hf = wf2[par], w2f2[par], wbf2[par], hfm2[par]
            t_x = ldt[t]
            wait(VE, sd, t_x)
            wait(SC, sd, t_x)
            t_l3 = ln_tile(wf[:, :], None, w2f[:, :], wbf[:, :])
            wait(SC, sv, t_l3)
            t_l3b = ln_tile(w2f[:, :], None, wf[:, :], wbf[:, :])
            wait(VE, sv, t_l3b)
            inc(VE, sv, lambda e, wf=wf: e.tensor_tensor(wf[:, :], wf[:, :], gam_rep[:, :], ALU.mult))
            wait(VE, sv, sv.n)
            t_h = inc(VE, sv, lambda e, wf=wf, wbf=wbf: e.tensor_tensor(
                wbf[:, :], wf[:, :], bet_rep[:, :], ALU.add))
            t_hf_v, t_hf_s = transpose_tile(wbf, lambda ch, hf=hf: hf[:, ch * 128:(ch + 1) * 128],
                                            t_h, alt=bool(KB16 & 8))
            b7tok["v"], b7tok["s"] = t_hf_v, t_hf_s
            if t + 1 < NTB:
                b0_load(t + 1)
            wait(SY, sv, t_hf_v)
            wait(SY, ss, t_hf_s)
            t_hw = dma(SY, sd, hfm_d[t * 128:(t + 1) * 128, :], hf[:, :])
            pmark(par)

        def proj_from(hbuf, brep_, wslot, dep_sd, out_f, wtoks=None, rec=None):
            lastw = 0
            for sl in range(c.NSL):
                bank = sl % 4
                if wtoks is not None:
                    wait(TE, sg, wtoks[sl])
                wait(TE, sd, dep_sd)
                wait(TE, sv, pbank_v[bank])
                wait(TE, ss, pbank_s[bank])
                for ch in range(c.CH):
                    inc(TE, st, lambda e, sl=sl, ch=ch, bank=bank, hbuf=hbuf, wslot=wslot: e.matmul(
                        pb[bank][:, 0:c.SW],
                        hbuf[:, ch * 128:(ch + 1) * 128],
                        wslot[:, ch * c.D + sl * c.SW: ch * c.D + sl * c.SW + c.SW],
                        start=(ch == 0), stop=(ch == c.CH - 1)))
                if rec is not None:
                    rec[sl] = st.n
                wait(VE, st, st.n)
                lastw = inc(VE, sv, lambda e, sl=sl, bank=bank, out_f=out_f, brep_=brep_:
                            e.tensor_tensor(out_f[:, sl * c.SW:(sl + 1) * c.SW],
                                            pb[bank][:, 0:c.SW],
                                            brep_[:, sl * c.SW:(sl + 1) * c.SW], ALU.add))
                pbank_v[bank] = lastw
            return lastw

        def cumsum_tile(src_bf, ydst, cl_i, dep_v, first_tile):
            lastv = lasts = 0
            for sl in range(c.NSL):
                wait(TE, sv, dep_v)
                wait(TE, sv, lastv)
                wait(TE, ss, lasts)
                inc(TE, st, lambda e, sl=sl, src_bf=src_bf: e.matmul(
                    pb[4][:, 0:c.SW], tri_bf[:, :], src_bf[:, sl * c.SW:(sl + 1) * c.SW],
                    start=True, stop=first_tile))
                if not first_tile:
                    if KB16 & 1:
                        inc(TE, st, lambda e, sl=sl: e.matmul(
                            pb[4][:, 0:c.SW], ones1_b[:, :],
                            tloc_rb[:, sl * c.SW:(sl + 1) * c.SW],
                            start=False, stop=True))
                    else:
                        inc(TE, st, lambda e, sl=sl: e.matmul(
                            pb[4][:, 0:c.SW], ones1_f[:, :],
                            cloc_ap(0)[:, sl * c.SW:(sl + 1) * c.SW],
                            start=False, stop=True))
                t_y = st.n
                inc(TE, st, lambda e, sl=sl, src_bf=src_bf: e.matmul(
                    pb[5][0:1, 0:c.SW], ones128_bf[:, :], src_bf[:, sl * c.SW:(sl + 1) * c.SW],
                    start=True, stop=True))
                t_cs = st.n
                wait(SC, st, t_y)
                lasts = inc(SC, ss, lambda e, sl=sl, ydst=ydst: e.copy(
                    ydst[:, sl * c.SW:(sl + 1) * c.SW], pb[4][:, 0:c.SW]))
                wait(VE, st, t_cs)
                if first_tile:
                    wait(VE, scin, rowtok["tloc_free"])
                    lastv = inc(VE, sv, lambda e, sl=sl: e.tensor_copy(
                        cloc_ap(0)[:, sl * c.SW:(sl + 1) * c.SW], pb[5][0:1, 0:c.SW]))
                else:
                    lastv = inc(VE, sv, lambda e, sl=sl: e.tensor_tensor(
                        cloc_ap(0)[:, sl * c.SW:(sl + 1) * c.SW],
                        cloc_ap(0)[:, sl * c.SW:(sl + 1) * c.SW],
                        pb[5][0:1, 0:c.SW], ALU.add))
                wait(VE, sv, lastv)
                lastv = inc(VE, sv, lambda e, sl=sl: e.tensor_copy(
                    tloc_rb[:, sl * c.SW:(sl + 1) * c.SW],
                    cloc_ap(0)[:, sl * c.SW:(sl + 1) * c.SW]))
            return lasts, lastv

        def carry_round(cl_i, dep_v):
            wait(SY, sv, dep_v)
            wait(SY, scin, scin.n)
            t_w = dma(SY, scin, ccin[:, :], cloc_ap(cl_i))
            rowtok["tloc_free"] = t_w
            wait(GP, scin, t_w)
            t_cc = inc(GP, scc, lambda e: e.collective_compute(
                "AllGather", ALU.bypass, replica_groups=[list(range(NC_))],
                ins=[ccin.ap().opt()], outs=[ccout.ap().opt()]))
            wait(SY, scc, t_cc)
            t_r = dma(SY, scar, carry_sb[:, :], ccout[:, :])
            wait(TE, scar, t_r)
            wait(TE, sd, t_ld)
            last = 0
            for sl in range(c.NSL):
                wait(TE, ss, last)
                inc(TE, st, lambda e, sl=sl: e.matmul(
                    pb[7][0:1, 0:c.SW], prefix_sb[:, :], carry_sb[:, sl * c.SW:(sl + 1) * c.SW],
                    start=True, stop=True))
                wait(SC, st, st.n)
                last = inc(SC, ss, lambda e, sl=sl, cl_i=cl_i: e.copy(
                    crow_ap(cl_i)[:, sl * c.SW:(sl + 1) * c.SW], pb[7][0:1, 0:c.SW]))
                wait(SC, ss, last)
                last = inc(SC, ss, lambda e, sl=sl: e.copy(
                    carry_rb[:, sl * c.SW:(sl + 1) * c.SW],
                    crow_ap(cl_i)[:, sl * c.SW:(sl + 1) * c.SW]))
            b7tok["s"] = last
            return last

        def add_carry(y_f32, cl_i, dep_s, dep_sd_v):
            lastv = 0
            wait(TE, sv, b7tok["v"])
            wait(TE, ss, b7tok["s"])
            for sl in range(c.NSL):
                wait(TE, ss, dep_s)
                wait(TE, sv, lastv)
                if KB16 & 2:
                    inc(TE, st, lambda e, sl=sl: e.matmul(
                        pb[7][:, 0:c.SW], ones1_b[:, :], carry_rb[:, sl * c.SW:(sl + 1) * c.SW],
                        start=True, stop=True))
                else:
                    inc(TE, st, lambda e, sl=sl: e.matmul(
                        pb[7][:, 0:c.SW], ones1_f[:, :], crow_ap(0)[:, sl * c.SW:(sl + 1) * c.SW],
                        start=True, stop=True))
                wait(VE, st, st.n)
                wait(VE, sd, dep_sd_v)
                lastv = inc(VE, sv, lambda e, sl=sl, y_f32=y_f32: e.tensor_tensor(
                    y_f32[:, sl * c.SW:(sl + 1) * c.SW], y_f32[:, sl * c.SW:(sl + 1) * c.SW],
                    pb[7][:, 0:c.SW], ALU.add))
            b7tok["v"] = lastv
            return lastv

        if phases >= 5:
            fill_rep(brep, brows[0:1, :])
        t_wl = load_wbig(WqT) if phases >= 5 else None
        wq_free = [0] * c.NSL
        wqa_free = [0] * c.NSL
        wk_free = [0] * c.NSL
        wv_free = [0] * c.NSL
        pass_fence()

        def b1a_load(t):
            par = (t % 2) if KPAR else 0
            pwaits(par)
            ldt[t] = dma(SY, sd, hfm2[par][:, :], hfm_d[t * 128:(t + 1) * 128, :])
        if NTB:
            b1a_load(0)
        for t in range(NTB):
            par = (t % 2) if KPAR else 0
            wf, wbf, hf = wf2[par], wbf2[par], hfm2[par]
            t_h = ldt[t]
            t_q = proj_from(hf, brep, wbig, t_h, wf, t_wl,
                            rec=(wq_free if t == NTB - 1 else None))
            if t + 1 < NTB:
                b1a_load(t + 1)
            wait(SY, sv, t_q)
            t_qs = dma(SY, sd, qry_d[t * 128:(t + 1) * 128, :], wf[:, :])
            wait(VE, sv, t_q)
            t_qb = inc(VE, sv, lambda e, wf=wf, wbf=wbf: e.tensor_copy(wbf[:, :], wf[:, :]))
            t_fm_v, t_fm_s = transpose_tile(wbf, lambda ch, hf=hf: hf[:, ch * 128:(ch + 1) * 128],
                                            t_qb, alt=bool(KB16 & 16))
            b7tok["v"], b7tok["s"] = t_fm_v, t_fm_s
            wait(SY, sv, t_fm_v)
            wait(SY, ss, t_fm_s)
            t_qf = dma(SY, sd, qfm_d[t * 128:(t + 1) * 128, :], hf[:, :])
            pmark(par)

        t_wl = load_wbig(WqaT, wq_free) if phases >= 5 else None
        pass_fence()

        def b1b_load(t):
            par = (t % 2) if KPAR else 0
            pwaits(par)
            ldt[t] = dma(SY, sd, hfm2[par][:, :], qfm_d[t * 128:(t + 1) * 128, :])
        if NTB:
            b1b_load(0)
        cs_s = cs_v = 0
        for t in range(NTB):
            par = (t % 2) if KPAR else 0
            wf, w2f, wbf, hf = wf2[par], w2f2[par], wbf2[par], hfm2[par]
            t_h = ldt[t]
            lastw = 0
            for sl in range(c.NSL):
                bank = sl % 4
                if t_wl is not None:
                    wait(TE, sg, t_wl[sl])
                wait(TE, sd, t_h)
                wait(TE, sv, pbank_v[bank])
                wait(TE, ss, pbank_s[bank])
                for ch in range(c.CH):
                    inc(TE, st, lambda e, sl=sl, ch=ch, bank=bank, hf=hf: e.matmul(
                        pb[bank][:, 0:c.SW], hf[:, ch * 128:(ch + 1) * 128],
                        wbig[:, ch * c.D + sl * c.SW: ch * c.D + sl * c.SW + c.SW],
                        start=(ch == 0), stop=False))
                inc(TE, st, lambda e, sl=sl, bank=bank: e.matmul(
                    pb[bank][:, 0:c.SW], ones1_b[:, :],
                    bqa_rb[0:1, sl * c.SW:(sl + 1) * c.SW],
                    start=False, stop=True))
                if t == NTB - 1:
                    wqa_free[sl] = st.n
                wait(SC, st, st.n)
                lastw = inc(SC, ss, lambda e, sl=sl, bank=bank, wf=wf: e.activation(
                    wf[:, sl * c.SW:(sl + 1) * c.SW], pb[bank][:, 0:c.SW],
                    AF.Exp, scale=c.QASCL))
                pbank_s[bank] = lastw
            if t + 1 < NTB:
                b1b_load(t + 1)
            wait(SY, ss, lastw)
            t_qw = dma(SY, sd, qw_d[t * 128:(t + 1) * 128, :], wf[:, :])
            wait(VE, ss, lastw)
            t_qwb = inc(VE, sv, lambda e, wf=wf, wbf=wbf: e.tensor_copy(wbf[:, :], wf[:, :]))
            cs_s, cs_v = cumsum_tile(wbf, w2f, 0, t_qwb, t == 0)
            wait(SY, ss, cs_s)
            t_y1 = dma(SY, sd, y1_d[t * 128:(t + 1) * 128, :], w2f[:, :])
            pmark(par)
        t_c1 = carry_round(0, cs_v) if phases >= 5 else 0

        if phases >= 5:
            fill_rep(brep, brows[1:2, :])
        t_wl = load_wbig(WkT, wqa_free) if phases >= 5 else None
        pass_fence()

        def bkv_load(t):
            par = (t % 2) if KPAR else 0
            pwaits(par)
            ldt[t] = dma(SY, sd, hfm2[par][:, :], hfm_d[t * 128:(t + 1) * 128, :])
        if NTB:
            bkv_load(0)
        for t in range(NTB):
            par = (t % 2) if KPAR else 0
            wf, hf = wf2[par], hfm2[par]
            t_h = ldt[t]
            t_k = proj_from(hf, brep, wbig, t_h, wf, t_wl,
                            rec=(wk_free if t == NTB - 1 else None))
            if t + 1 < NTB:
                bkv_load(t + 1)
            wait(SY, sv, t_k)
            t_ks = dma(SY, sd, key_d[t * 128:(t + 1) * 128, :], wf[:, :])
            pmark(par)
        if phases >= 5:
            fill_rep(brep, brows[2:3, :])
        t_wl = load_wbig(WvT, wk_free) if phases >= 5 else None
        pass_fence()

        def bkv_load(t):
            par = (t % 2) if KPAR else 0
            pwaits(par)
            ldt[t] = dma(SY, sd, hfm2[par][:, :], hfm_d[t * 128:(t + 1) * 128, :])
        if NTB:
            bkv_load(0)
        for t in range(NTB):
            par = (t % 2) if KPAR else 0
            wf, hf = wf2[par], hfm2[par]
            t_h = ldt[t]
            t_k = proj_from(hf, brep, wbig, t_h, wf, t_wl,
                            rec=(wv_free if t == NTB - 1 else None))
            if t + 1 < NTB:
                bkv_load(t + 1)
            wait(SY, sv, t_k)
            t_ks = dma(SY, sd, val_d[t * 128:(t + 1) * 128, :], wf[:, :])
            pmark(par)

        pass_fence()

        def b3_load(t):
            par = (t % 2) if KPAR else 0
            pwaits(par)
            a = dma(SY, sd, wf2[par][:, :], qw_d[t * 128:(t + 1) * 128, :])
            b = dma(SY, sd, w2f2[par][:, :], y1_d[t * 128:(t + 1) * 128, :])
            ldt[t] = (a, b)
        if NTB:
            b3_load(0)
        cs_s = cs_v = 0
        for t in range(NTB):
            par = (t % 2) if KPAR else 0
            wf, w2f, wbf = wf2[par], w2f2[par], wbf2[par]
            t_qw, t_y1 = ldt[t]
            t_yg = add_carry(w2f, 0, t_c1, t_y1)
            wait(SC, sv, t_yg)
            inc(SC, ss, lambda e, w2f=w2f: e.activation(w2f[:, :], w2f[:, :], AF.Ln))
            wait(SC, ss, ss.n)
            t_rec = inc(SC, ss, lambda e, w2f=w2f: e.activation(w2f[:, :], w2f[:, :],
                                                                AF.Exp, scale=-1.0))
            wait(VE, ss, t_rec)
            wait(VE, sd, t_y1)
            t_qwn = inc(VE, sv, lambda e, wf=wf, w2f=w2f: e.tensor_tensor(
                wf[:, :], wf[:, :], w2f[:, :], ALU.mult))
            wait(SY, sv, t_qwn)
            t_qr = dma(SY, sd, w2f[:, :], qry_d[t * 128:(t + 1) * 128, :])
            wait(VE, sd, t_qr)
            t_t2 = inc(VE, sv, lambda e, wf=wf, w2f=w2f, wbf=wbf: e.tensor_tensor(
                wbf[:, :], wf[:, :], w2f[:, :], ALU.mult))
            if t + 1 < NTB:
                b3_load(t + 1)
            cs_s, cs_v = cumsum_tile(wbf, wf, 1, t_t2, t == 0)
            wait(SY, ss, cs_s)
            t_w = dma(SY, sd, y2_d[t * 128:(t + 1) * 128, :], wf[:, :])
            pmark(par)
        t_c2 = carry_round(1, cs_v) if phases >= 5 else 0

        t_wl = load_wbig(WkaT, wv_free) if phases >= 5 else None
        pass_fence()

        def b4_load(t):
            par = (t % 2) if KPAR else 0
            pwaits(par)
            a = dma(SY, sd, wf2[par][:, :], y2_d[t * 128:(t + 1) * 128, :])
            b = dma(SY, sd, w2f2[par][:, :], key_d[t * 128:(t + 1) * 128, :])
            ldt[t] = (a, b)
        if NTB:
            b4_load(0)
        cs_s = cs_v = 0
        for t in range(NTB):
            par = (t % 2) if KPAR else 0
            wf, w2f, wbf, hf = wf2[par], w2f2[par], wbf2[par], hfm2[par]
            t_y2, t_k = ldt[t]
            t_pq = add_carry(wf, 1, t_c2, t_k)
            wait(VE, sd, t_k)
            wait(VE, sv, t_pq)
            t_mx = inc(VE, sv, lambda e, wf=wf, w2f=w2f: e.tensor_tensor(
                wf[:, :], wf[:, :], w2f[:, :], ALU.mult))
            wait(SY, sv, t_mx)
            t_mw = dma(SY, sd, mix_d[t * 128:(t + 1) * 128, :], wf[:, :])
            wait(VE, sv, t_mx)
            t_mb = inc(VE, sv, lambda e, wf=wf, wbf=wbf: e.tensor_copy(wbf[:, :], wf[:, :]))
            t_mfm_v, t_mfm_s = transpose_tile(wbf, lambda ch, hf=hf: hf[:, ch * 128:(ch + 1) * 128],
                                              t_mb, alt=bool(KB16 & 32))
            b7tok["v"], b7tok["s"] = t_mfm_v, t_mfm_s
            wait(SC, sd, t_mw)
            lastw = 0
            for sl in range(c.NSL):
                bank = sl % 4
                if t_wl is not None:
                    wait(TE, sg, t_wl[sl])
                wait(TE, sv, t_mfm_v)
                wait(TE, ss, t_mfm_s)
                wait(TE, sv, pbank_v[bank])
                wait(TE, ss, pbank_s[bank])
                for ch in range(c.CH):
                    inc(TE, st, lambda e, sl=sl, ch=ch, bank=bank, hf=hf: e.matmul(
                        pb[bank][:, 0:c.SW], hf[:, ch * 128:(ch + 1) * 128],
                        wbig[:, ch * c.D + sl * c.SW: ch * c.D + sl * c.SW + c.SW],
                        start=(ch == 0), stop=False))
                inc(TE, st, lambda e, sl=sl, bank=bank: e.matmul(
                    pb[bank][:, 0:c.SW], ones1_b[:, :],
                    bka_rb[0:1, sl * c.SW:(sl + 1) * c.SW],
                    start=False, stop=True))
                wait(SC, st, st.n)
                lastw = inc(SC, ss, lambda e, sl=sl, bank=bank, wf=wf: e.activation(
                    wf[:, sl * c.SW:(sl + 1) * c.SW], pb[bank][:, 0:c.SW],
                    AF.Exp, scale=c.QASCL))
                pbank_s[bank] = lastw
            if t + 1 < NTB:
                b4_load(t + 1)
            wait(SY, ss, lastw)
            t_kw = dma(SY, sd, kw_d[t * 128:(t + 1) * 128, :], wf[:, :])
            wait(VE, ss, lastw)
            t_kwb = inc(VE, sv, lambda e, wf=wf, wbf=wbf: e.tensor_copy(wbf[:, :], wf[:, :]))
            cs_s, cs_v = cumsum_tile(wbf, w2f, 2, t_kwb, t == 0)
            wait(SY, ss, cs_s)
            t_y3 = dma(SY, sd, y3_d[t * 128:(t + 1) * 128, :], w2f[:, :])
            pmark(par)
        t_c3 = carry_round(2, cs_v) if phases >= 5 else 0

        pass_fence()

        def b5a_load(t):
            par = (t % 2) if KPAR else 0
            pwaits(par)
            a = dma(SY, sd, wf2[par][:, :], kw_d[t * 128:(t + 1) * 128, :])
            b = dma(SY, sd, w2f2[par][:, :], y3_d[t * 128:(t + 1) * 128, :])
            ldt[t] = (a, b)
        if NTB:
            b5a_load(0)
        cs_s = cs_v = 0
        for t in range(NTB):
            par = (t % 2) if KPAR else 0
            wf, w2f, wbf = wf2[par], w2f2[par], wbf2[par]
            t_kw, t_y3 = ldt[t]
            t_yg = add_carry(w2f, 2, t_c3, t_y3)
            wait(SC, sv, t_yg)
            inc(SC, ss, lambda e, w2f=w2f: e.activation(w2f[:, :], w2f[:, :], AF.Ln))
            wait(SC, ss, ss.n)
            t_rec = inc(SC, ss, lambda e, w2f=w2f: e.activation(w2f[:, :], w2f[:, :],
                                                                AF.Exp, scale=-1.0))
            wait(VE, ss, t_rec)
            wait(VE, sd, t_y3)
            t_kwn = inc(VE, sv, lambda e, wf=wf, w2f=w2f: e.tensor_tensor(
                wf[:, :], wf[:, :], w2f[:, :], ALU.mult))
            wait(SY, sv, t_kwn)
            t_mx = dma(SY, sd, w2f[:, :], mix_d[t * 128:(t + 1) * 128, :])
            wait(VE, sd, t_mx)
            t_t4 = inc(VE, sv, lambda e, wf=wf, w2f=w2f, wbf=wbf: e.tensor_tensor(
                wbf[:, :], wf[:, :], w2f[:, :], ALU.mult))
            if t + 1 < NTB:
                b5a_load(t + 1)
            cs_s, cs_v = cumsum_tile(wbf, wf, 3, t_t4, t == 0)
            wait(SY, ss, cs_s)
            t_w = dma(SY, sd, y4_d[t * 128:(t + 1) * 128, :], wf[:, :])
            pmark(par)
        t_c4 = carry_round(3, cs_v) if phases >= 5 else 0

        pass_fence()

        def b5b_load(t):
            par = (t % 2) if KPAR else 0
            pwaits(par)
            a = dma(SY, sd, wf2[par][:, :], y4_d[t * 128:(t + 1) * 128, :])
            b = dma(SY, sd, w2f2[par][:, :], val_d[t * 128:(t + 1) * 128, :])
            ldt[t] = (a, b)
        if NTB:
            b5b_load(0)
        for t in range(NTB):
            par = (t % 2) if KPAR else 0
            wf, w2f, wbf = wf2[par], w2f2[par], wbf2[par]
            t_y4, t_v = ldt[t]
            t_pk = add_carry(wf, 3, t_c4, t_v)
            wait(VE, sd, t_v)
            wait(VE, sv, t_pk)
            t_ob = inc(VE, sv, lambda e, wf=wf, w2f=w2f: e.tensor_tensor(
                wf[:, :], wf[:, :], w2f[:, :], ALU.mult))
            wait(SY, sv, t_ob)
            t_xs = dma(SY, sd, w2f[:, :], xs2_d[t * 128:(t + 1) * 128, :])
            wait(VE, sd, t_xs)
            t_x3 = inc(VE, sv, lambda e, wf=wf, w2f=w2f: e.tensor_tensor(
                wf[:, :], w2f[:, :], wf[:, :], ALU.add))
            wait(SY, sv, t_x3)
            t_w3 = dma(SY, sd, xs3_d[t * 128:(t + 1) * 128, :], wf[:, :])
            wait(SC, sv, t_x3)
            if t + 1 < NTB:
                b5b_load(t + 1)
            t_l4 = ln_tile(wf[:, :], wbf[:, :], None, wbf[:, :])
            t_l4v, t_l4s = transpose_tile(wbf, lambda ch, t=t: lnx_reg[ch][:, t * 128:(t + 1) * 128],
                                          t_l4, alt=bool(KB16 & 64))
            b7tok["v"], b7tok["s"] = t_l4v, t_l4s
            wait(SY, sd, t_w3)
            pmark(par)
        b_v = sv.n
        b_s = ss.n
        A.pop()


        if phases >= 6:
            ffn_phase(b_v, b_s, xs3_d, y_out)

        SY.raw(lambda e: e.wait_ge(sd.h, sd.n))
        SY.raw(lambda e: e.wait_ge(sg.h, sg.n))

        @block.sync
        def _(eng):
            for f in SY.ops:
                f(eng)

        @block.gpsimd
        def _(eng):
            for f in GP.ops:
                f(eng)

        @block.tensor
        def _(eng):
            for f in TE.ops:
                f(eng)

        @block.vector
        def _(eng):
            for f in VE.ops:
                f(eng)

        @block.scalar
        def _(eng):
            for f in SC.ops:
                f(eng)

    return nc


def make_masks():
    f32 = np.float32
    ar = np.arange
    maske = ((ar(128)[:, None] + (ar(2 * QG)[None, :] // QG) * 128)
             <= (ar(2 * QG)[None, :] % QG)).astype(f32)
    masko = ((ar(128)[:, None] + (ar(4 * QG)[None, :] // QG) * 128)
             <= (QG + ar(4 * QG)[None, :] % QG)).astype(f32)
    return maske, masko


def make_in_maps(cfg, inp):
    c = cfg
    f32 = np.float32
    x = np.asarray(inp["L0_attn_a_x"], f32).reshape(c.T, c.D)
    wqT = np.ascontiguousarray(np.asarray(inp["L0_attn_a_wq"], f32).T)
    wkT = np.ascontiguousarray(np.asarray(inp["L0_attn_a_wk"], f32).T)
    wvT = np.ascontiguousarray(np.asarray(inp["L0_attn_a_wv"], f32).T)
    maske, masko = make_masks()
    brows = np.stack([
        np.asarray(inp["L0_attn_b_bq"], f32), np.asarray(inp["L0_attn_b_bk"], f32),
        np.asarray(inp["L0_attn_b_bv"], f32), np.asarray(inp["L0_attn_b_bqa"], f32),
        np.asarray(inp["L0_attn_b_bka"], f32)])
    gbrow = np.stack([np.asarray(inp["L0_attn_b_ln_g"], f32),
                      np.asarray(inp["L0_attn_b_ln_b"], f32)])
    common = {
        "x_full": x,
        "tau_in": np.asarray(inp["L0_attn_a_tau"], f32).reshape(1, 1),
        "nu_in": np.asarray(inp["L0_attn_a_nu"], f32).reshape(1, 1),
        "WqT": np.ascontiguousarray(np.asarray(inp["L0_attn_b_Wq"], f32).T),
        "WkT": np.ascontiguousarray(np.asarray(inp["L0_attn_b_Wk"], f32).T),
        "WvT": np.ascontiguousarray(np.asarray(inp["L0_attn_b_Wv"], f32).T),
        "WqaT": np.ascontiguousarray(np.asarray(inp["L0_attn_b_Wqa"], f32).T),
        "WkaT": np.ascontiguousarray(np.asarray(inp["L0_attn_b_Wka"], f32).T),
        "brows": brows, "gbrow": gbrow,
        "W_g": np.asarray(inp["L0_ffn_W_g"], f32),
        "We_gate": np.asarray(inp["L0_ffn_We_gate"], f32),
        "We_up": np.asarray(inp["L0_ffn_We_up"], f32),
        "We_down": np.asarray(inp["L0_ffn_We_down"], f32),
        "Ws_gate": np.asarray(inp["L0_ffn_Ws_gate"], f32),
        "Ws_up": np.asarray(inp["L0_ffn_Ws_up"], f32),
        "Ws_down": np.asarray(inp["L0_ffn_Ws_down"], f32),
        "tri_in": np.triu(np.ones((128, 128), f32)),
        "ident_in": np.eye(128, dtype=f32),
        "maske_in": maske, "masko_in": masko,
    }
    maps = []
    for cc in range(NC_):
        d = dict(common)
        d["x_loc"] = np.ascontiguousarray(x[cc * c.TL:(cc + 1) * c.TL])
        h0 = cc * HPC
        d["wqh"] = np.ascontiguousarray(wqT[:, h0 * c.DH:(h0 + HPC) * c.DH])
        d["wkh"] = np.ascontiguousarray(wkT[:, h0 * c.DH:(h0 + HPC) * c.DH])
        d["wvh"] = np.ascontiguousarray(wvT[:, h0 * c.DH:(h0 + HPC) * c.DH])
        pv = np.zeros((NC_, 1), f32)
        pv[:cc] = 1.0
        d["prefix_in"] = pv
        gi = np.zeros((128, c.NT * NC_), np.int32)
        for t in range(c.NT):
            for r in range(NC_):
                gi[:, t * NC_ + r] = r * c.T + cc * c.TL + t * 128 + np.arange(128)
        d["gidx_in"] = gi
        maps.append(d)
    return maps


def kernel(**inputs):
    cfg = REAL
    nc = build(cfg)
    in_maps = make_in_maps(cfg, inputs)
    res = run_bass_kernel_spmd(nc, in_maps, core_ids=list(range(NC_)))
    out = np.concatenate([np.asarray(res.results[cc]["y"]) for cc in range(NC_)], axis=0)
    return out.reshape(1, cfg.T, cfg.D).astype(np.float32)
